# version 19 (final)
# baseline (speedup 1.0000x reference)
import numpy as np
import concourse.bass as bass
import concourse.mybir as mybir
from concourse.bass_utils import run_bass_kernel_spmd

F32 = mybir.dt.float32
BF16 = mybir.dt.bfloat16
U8 = mybir.dt.uint8
AF = mybir.ActivationFunctionType
ALU = mybir.AluOpType
AX = mybir.AxisListType.X

NCORES = 8
B, L, D = 4, 2048, 2048
T = 1024
NK = D // 128
H = 5632
NM = H // 128
NH = 16
E, R = 8, 16
ER = E * R
ISCALE = float(1.0 / np.sqrt(128.0))
EPS = 1e-5
GROUPS = [[0, 1], [2, 3], [4, 5], [6, 7]]
NGRP = 4
GM = NM // NGRP

ALL_PARTS = ("ffn1", "attn", "ffn2")


class Op:
    __slots__ = ("stream", "fn", "deps", "selfdeps", "gidx", "marked", "kind",
                 "dma_sem", "dma_cnt", "tag", "mcount", "sidx")

    def __init__(self, stream, fn, kind="c", tag=""):
        self.stream = stream
        self.fn = fn
        self.deps = []
        self.selfdeps = []
        self.marked = False
        self.kind = kind
        self.dma_sem = None
        self.dma_cnt = 0
        self.tag = tag


class Em:
    STREAMS = ("pe", "act", "dve", "pool", "sp")

    def __init__(self):
        self.ops = {s: [] for s in self.STREAMS}
        self.glist = []
        self.res = {}
        self.dma_counts = {}
        self.cc_counts = {}
        self.resmap = {}

    def _expand(self, names):
        out = []
        for n in names:
            out.extend(self.resmap.get(n, (n,)))
        return out

    def _res(self, name):
        st = self.res.get(name)
        if st is None:
            st = {"writers": {}, "readers": {}, "xcl": name.startswith("P:")}
            self.res[name] = st
        return st

    def _add(self, op, reads, writes):
        op.gidx = len(self.glist)
        op.sidx = len(self.ops[op.stream])
        self.glist.append(op)
        self.ops[op.stream].append(op)
        dep = {}

        selfdep = {}

        def add_dep(o, selfok=False):
            if o is op:
                return
            if o.kind == "c" and o.stream == op.stream:
                if selfok and op.stream != "pe" and op.kind == "c":
                    selfdep[id(o)] = o
                return
            dep[id(o)] = o

        for r in reads:
            st = self._res(r)
            if st["xcl"]:
                for o in list(st["writers"].values()) + list(st["readers"].values()):
                    add_dep(o, selfok=True)
            else:
                for o in st["writers"].values():
                    add_dep(o, selfok=True)
        for w in writes:
            st = self._res(w)
            for o in st["writers"].values():
                add_dep(o, selfok=True)
            for o in st["readers"].values():
                add_dep(o, selfok=True)
        op.deps = list(dep.values())
        op.selfdeps = list(selfdep.values())
        skey = op.stream if op.kind == "c" else ("%s#%s" % (op.kind, op.dma_sem))
        for r in reads:
            self._res(r)["readers"][skey] = op
        for w in writes:
            st = self._res(w)
            st["writers"] = {skey: op}
            st["readers"] = {}
        return op

    def op(self, stream, fn, reads=(), writes=(), tag=""):
        return self._add(Op(stream, fn, "c", tag), self._expand(reads),
                         self._expand(writes))

    def dma(self, stream, sem, fn, reads=(), writes=(), tag=""):
        o = Op(stream, fn, "d", tag)
        o.dma_sem = sem
        self.dma_counts[sem] = self.dma_counts.get(sem, 0) + 1
        o.dma_cnt = self.dma_counts[sem]
        return self._add(o, self._expand(reads), self._expand(writes))

    def cc(self, sem, fn, reads=(), writes=(), tag=""):
        o = Op("pool", fn, "cc", tag)
        o.dma_sem = sem
        self.cc_counts[sem] = self.cc_counts.get(sem, 0) + 1
        o.dma_cnt = self.cc_counts[sem]
        return self._add(o, self._expand(reads), self._expand(writes))

    SELFDIST = 8

    def finalize(self):
        for o in self.glist:
            for d in o.deps:
                d.marked = True
            for d in o.selfdeps:
                if o.sidx - d.sidx <= self.SELFDIST:
                    d.marked = True
        for s in self.STREAMS:
            c = 0
            for o in self.ops[s]:
                if o.kind == "c" and o.marked:
                    c += 1
                o.mcount = c

        def ckey(o):
            return ("s", o.stream) if o.kind == "c" else (o.kind, o.dma_sem)

        def cval(o):
            return o.mcount if o.kind == "c" else o.dma_cnt

        clocks = [None] * len(self.glist)
        prev = {s: None for s in self.STREAMS}
        self.waits = [[] for _ in self.glist]
        for o in self.glist:
            clk = {}
            p = prev[o.stream]
            if p is not None:
                clk.update(clocks[p.gidx])
            best = {}
            for d in o.deps:
                k, v = ckey(d), cval(d)
                if clk.get(k, 0) >= v:
                    continue
                cur = best.get(k)
                if cur is None or v > cur[1]:
                    best[k] = (d, v)
            selfv = 0
            for d in o.selfdeps:
                if o.sidx - d.sidx <= self.SELFDIST:
                    selfv = max(selfv, d.mcount)
            if selfv > 0:
                self.waits[o.gidx].append((("s", o.stream), selfv))
            for k, (d, v) in best.items():
                self.waits[o.gidx].append((k, v))
                for kk, vv in clocks[d.gidx].items():
                    if clk.get(kk, 0) < vv:
                        clk[kk] = vv
                if clk.get(k, 0) < v:
                    clk[k] = v
            if o.kind == "c" and o.marked:
                k, v = ckey(o), cval(o)
                if clk.get(k, 0) < v:
                    clk[k] = v
            clocks[o.gidx] = clk
            prev[o.stream] = o


def build_nc(parts=ALL_PARTS, debug=False):
    nc = bass.Bass("TRN2", target_bir_lowering=False)
    d_in = {}

    def din(name, shape):
        d_in[name] = nc.dram_tensor(name, list(shape), F32, kind="ExternalInput").ap()
        return d_in[name]

    x_fm = din("x_fm", (128, NK * T))
    gwt = din("gwt", (128, NK, 8))
    uat = din("uat", (128, NK, ER))
    gat = din("gat", (128, NK, ER))
    ffnb = din("ffnb", (NM, 128, 32, 128))
    ffnl = din("ffnl", (NM, 128, 3, 128))
    wdt = din("wdt", (NGRP, NK, 128, GM, 128))
    dbt = din("dbt", (128, NK, 128))
    c_ident = din("c_ident", (128, 128))
    c_expand = din("c_expand", (8, 128))
    if "attn" in parts:
        wqt = din("wqt", (NK, 128, NK, 128))
        wut = din("wut", (NK, 128, NK, 128))
        wot = din("wot", (NK, 128, NK, 128))
        kvslab_d = din("kvslab", (NH, 128, 32, 128))

    y_out = nc.dram_tensor("y_out", [128, NK * T], F32, kind="ExternalOutput").ap()
    dbg_out = nc.dram_tensor("dbg_out", [128, 64], F32, kind="ExternalOutput").ap() if debug else None

    cc_bufs = {}
    for nm, shape in [("st1i", (1, 4)), ("st1o", (2, 4)),
                      ("st2i", (1, 4)), ("st2o", (2, 4)),
                      ("kvi", (128, 2048)), ("kvo", (128, 2048))]:
        cc_bufs[nm] = nc.dram_tensor("cc_" + nm, list(shape), F32, kind="Internal")

    ffnb_bf = nc.dram_tensor("ffnb_bf", [NM, 128, 32, 128], BF16,
                             kind="Internal").ap()
    ffnl_bf = nc.dram_tensor("ffnl_bf", [NM, 128, 3, 128], BF16,
                             kind="Internal").ap()
    wdt_bf = nc.dram_tensor("wdt_bf", [NGRP, NK, 128, GM, 128], BF16,
                            kind="Internal").ap()

    em = Em()

    def S(name, shape, dtype):
        return nc.alloc_sbuf_tensor(name, list(shape), dtype)

    y = S("y", (128, NK * T), F32)
    xn = S("xn", (128, NK * T), BF16)
    slab = S("slab", (128, 2 * 32 * 128), BF16)
    lslab = S("lslab", (128, 4 * 3 * 128), BF16)
    db_sb = S("db_sb", (128, NK * 128), BF16)
    identf = S("identf", (128, 128), F32)
    identb = S("identb", (128, 128), BF16)
    expb = S("expb", (8, 128), BF16)
    ones1f = S("ones1f", (1, 128), F32)
    ones1b = S("ones1b", (1, 128), BF16)
    ones128b = S("ones128b", (128, 1), BF16)
    gw_sb = S("gw_sb", (128, 128), BF16)
    g2s = S("g2s", (128, 8), F32)
    v1a = S("v1a", (128, 8), F32)
    v2a = S("v2a", (128, 8), F32)
    oh1 = S("oh1", (128, 64), BF16)
    oh2 = S("oh2", (128, 64), BF16)
    e_scr = S("e_scr", (128, 8), F32)
    colscr = S("colscr", (128, 4), F32)
    stat4 = S("stat4", (1, 4), F32)
    rA = S("rA", (1, 4), F32)
    rB = S("rB", (1, 4), F32)
    dbg_sb = S("dbg_sb", (128, 64), F32) if debug else None
    epsb = S("epsb", (128, 1), F32)

    CH = 2048
    ARENA_BYTES = 88320
    arena = nc.alloc_sbuf_tensor("arena", [128, ARENA_BYTES], U8)
    abase = nc.lookup_mloc(arena).addr

    def A(name, shape, dtype, off, reskey=None):
        per = int(np.prod(shape[1:])) * mybir.dt.size(dtype)
        assert off % 32 == 0 and off + per <= ARENA_BYTES, (name, off, per)
        t = nc.alloc_sbuf_tensor_at(name, list(shape), dtype, offset=abase + off)
        c0, c1 = off // CH, (off + per - 1) // CH
        em.resmap[reskey or name] = tuple("AR:%d" % c for c in range(c0, c1 + 1))
        return t

    mixed = A("mixed", (128, GM * T), BF16, 0, reskey="_mixedfull")
    for j in range(GM):
        em.resmap["mixed:%d" % j] = ("AR:%d" % j,)
    tmpa = A("tmpa", (128, T), F32, 0)
    tmpb = A("tmpb", (128, T), BF16, 0)
    tmpb2 = A("tmpb2", (128, T), BF16, 2048)
    gfm = A("gfm", (8, T), F32, 4096)
    ua_sb = A("ua_sb", (128, 2048), BF16, 8192)
    ga_sb = A("ga_sb", (128, 2048), BF16, 12288)
    ohfm = A("ohfm", (8, 2 * T), BF16, 16384)
    ptm = A("ptm", (128, 4 * T), BF16, 22528, reskey="_ptmfull")
    for i in range(4):
        em.resmap["ptm:%d" % i] = ("AR:%d" % (11 + i),)
    a_b = A("a_b", (128, 2 * T), BF16, 30720)
    mask16 = A("mask16", (128, 2 * T), BF16, 34816)
    rows = A("rows", (1, 3 * T), F32, 38912)
    arow = A("arow", (1, 2 * T), F32, 38912, reskey="arow")
    rstd_b = A("rstd_b", (128, T), F32, 51200)
    mrs_b = A("mrs_b", (128, T), F32, 55296)
    bub = A("bub", (128, 2 * 512), BF16, 59392)
    bg = A("bg", (128, 512), F32, 61440)
    us1 = A("us1", (128, 512), BF16, 63488)
    gs1 = A("gs1", (128, 512), BF16, 65536)
    ss0 = A("ss0", (128, 512), BF16, 67584)
    ss1 = A("ss1", (128, 512), BF16, 69632)
    tsc = A("tsc", (128, 512), BF16, 71680)
    ybf = A("ybf", (128, 2 * T), BF16, 59392, reskey="ybf")
    ysq = A("ysq", (128, 2 * T), BF16, 63488, reskey="ysq")
    bg2 = A("bg2", (128, 512), F32, 38912)
    h02 = A("h02", (128, 512), BF16, 40960)
    q_acc = A("q_acc", (128, 2 * T), F32, 43008)
    g0b = A("g0b", (128, 2 * 512), BF16, 57344)
    wdslab2 = A("wdslab2", (128, 2 * GM * 128), BF16, 51200, reskey="_wdslab2full")
    em.resmap["wdslab:3"] = ("AR:25", "AR:26")
    em.resmap["wdslab:4"] = ("AR:26", "AR:27")
    srow = A("srow", (1, 16), F32, 56832)
    srow2 = A("srow2", (1, 16), F32, 56896)
    gt = A("gt", (128, 64), F32, 36864)
    h0 = A("h0", (128, 512), BF16, 77824)
    h1 = A("h1", (128, 512), BF16, 78848)
    wdslab = A("wdslab", (128, 3 * GM * 128), BF16, 79872, reskey="_wdslabfull")
    em.resmap["wdslab:0"] = ("AR:39", "AR:40")
    em.resmap["wdslab:1"] = ("AR:40", "AR:41")
    em.resmap["wdslab:2"] = ("AR:41", "AR:42", "AR:43")
    p_up = A("p_up", (128, T), BF16, 73728)
    p_gt = A("p_gt", (128, T), BF16, 75776)
    qt1 = A("qt1", (128, T), BF16, 73728, reskey="qt1")
    q_sb = A("q_sb", (128, T), BF16, 75776, reskey="q_sb")
    if "attn" in parts:
        oub = A("oub", (128, NK * T), BF16, 0, reskey="_oubfull")
        for hh in range(NK):
            em.resmap["oub:%d" % hh] = ("AR:%d" % hh,)
        kvbf = A("kvbf", (128, NH * 128), BF16, 32768)

        qfm = A("qfm", (128, 8 * T), BF16, 38912, reskey="_qfmfull")
        for hw in range(8):
            em.resmap["qfm:%d" % hw] = ("AR:%d" % (19 + hw),)
        kfm = A("kfm", (128, T), BF16, 55296)
        vfm = A("vfm", (128, T), BF16, 57344)
        ktm = A("ktm", (128, T), BF16, 59392)
        vtm = A("vtm", (128, T), BF16, 61440)
        kvsb = A("kvsb", (128, NH * 128), F32, 63488)
        u8 = A("u8", (128, 8 * T), BF16, 55296, reskey="_u8full")
        for hw in range(8):
            em.resmap["u8:%d" % hw] = ("AR:%d" % (27 + hw),)
        obf3 = A("obf3", (128, 2 * T), BF16, 71680, reskey="_obf3")
        em.resmap["obf:0"] = ("AR:35",)
        em.resmap["obf:1"] = ("AR:36",)
        rbb = A("rbb", (128, T), BF16, 75776)
        o2x2 = A("o2x2", (128, 2 * T), BF16, 77824, reskey="_o2x2")
        em.resmap["o2:0"] = ("AR:38",)
        em.resmap["o2:1"] = ("AR:39",)
        nr = A("nr", (1, T), F32, 81920)
        rrb = A("rrb", (1, T), BF16, 86016)

    ps01 = nc.alloc_psum_tensor("ps01", [128, 1024], F32)
    ps23 = nc.alloc_psum_tensor("ps23", [128, 1024], F32)
    ps45 = nc.alloc_psum_tensor("ps45", [128, 1024], F32)
    psT = nc.alloc_psum_tensor("psT", [128, 1024], BF16)
    ps7 = nc.alloc_psum_tensor("ps7", [128, 512], F32)
    PS = {"ps01": ps01, "ps23": ps23, "ps45": ps45, "psT": psT, "ps7": ps7}
    BANKBASE = {"ps01": 0, "ps23": 2, "ps45": 4, "psT": 6, "ps7": 7}

    def pbanks(t, c0, c1):
        bw = 1024 if t == "psT" else 512
        return ["P:%d" % (BANKBASE[t] + b) for b in range(c0 // bw, (c1 - 1) // bw + 1)]

    def MM(pt_, p0, p1, c0, c1, lhsT, rhs, start, stop, reads, tag=""):
        tns = PS[pt_]
        em.op("pe",
              lambda e, tns=tns, p0=p0, p1=p1, c0=c0, c1=c1, lhsT=lhsT, rhs=rhs,
              start=start, stop=stop:
              e.matmul(tns[p0:p1, c0:c1], lhsT, rhs, start=start, stop=stop),
              reads=reads, writes=pbanks(pt_, c0, c1), tag=tag)

    def TRANS(pt_, p0, p1, c0, c1, in_ap, ident, reads, tag=""):
        tns = PS[pt_]
        em.op("pe",
              lambda e, tns=tns, p0=p0, p1=p1, c0=c0, c1=c1, in_ap=in_ap, ident=ident:
              e.transpose(tns[p0:p1, c0:c1], in_ap, ident),
              reads=reads, writes=pbanks(pt_, c0, c1), tag=tag)

    YRES = ["y:%d" % k for k in range(NK)]

    em.dma("sp", "m_x", lambda e: e.dma_start(
        out=y[:, :], in_=x_fm),
        writes=YRES, tag="loadx")
    em.dma("sp", "m_c", lambda e: e.dma_start(out=identf[:, :], in_=c_ident),
           writes=["identf"])
    em.dma("pool", "m_c2", lambda e: e.dma_start(out=identb[:, :], in_=c_ident),
           writes=["identb"])
    em.dma("pool", "m_c3", lambda e: e.dma_start(out=expb[:, :], in_=c_expand),
           writes=["expb"])
    em.op("dve", lambda e: e.memset(ones1f[:, :], 1.0), writes=["ones1f"])
    em.op("dve", lambda e: e.memset(ones1b[:, :], 1.0), writes=["ones1b"])
    em.op("dve", lambda e: e.memset(epsb[:, :], EPS), writes=["epsb"])
    em.op("dve", lambda e: e.memset(ones128b[:, :], 1.0), writes=["ones128b"])
    em.dma("pool", "m_gw", lambda e: e.dma_start(
        out=gw_sb[:, :], in_=gwt), writes=["gw_sb"])
    em.dma("pool", "m_db", lambda e: e.dma_start(
        out=db_sb[:, :], in_=dbt), writes=["db_sb"])

    def emit_ln(tag):
        for k in range(NK):
            yr = y[:, k * T:(k + 1) * T]
            so_ = (k % 2) * T
            em.op("dve", lambda e, yr=yr, so_=so_: e.tensor_copy(
                  ybf[:, so_:so_ + T], yr),
                  reads=["y:%d" % k], writes=["ybf"], tag=tag)
            em.op("act", lambda e, yr=yr, so_=so_: e.activation(
                  ysq[:, so_:so_ + T], yr, AF.Square),
                  reads=["y:%d" % k], writes=["ysq"], tag=tag)
            for hf in range(2):
                MM("ps01", 0, 1, hf * 512, hf * 512 + 512, ones128b[:, :],
                   ybf[:, so_ + hf * 512:so_ + hf * 512 + 512], k == 0, k == NK - 1,
                   reads=["ones128b", "ybf"], tag=tag + "sum")
                MM("ps23", 0, 1, hf * 512, hf * 512 + 512, ones128b[:, :],
                   ysq[:, so_ + hf * 512:so_ + hf * 512 + 512], k == 0, k == NK - 1,
                   reads=["ones128b", "ysq"], tag=tag + "sq")
        r0 = rows[0:1, 0:T]
        r1 = rows[0:1, T:2 * T]
        r2 = rows[0:1, 2 * T:3 * T]
        em.op("dve", lambda e: e.tensor_scalar_mul(r0, ps01[0:1, :], 1.0 / D),
              reads=["P:0", "P:1"], writes=["rows"], tag=tag)
        em.op("dve", lambda e: e.tensor_mul(r2, r0, ps01[0:1, :]),
              reads=["P:0", "P:1", "rows"], writes=["rows"], tag=tag)
        em.op("dve", lambda e: e.tensor_sub(r2, ps23[0:1, :], r2),
              reads=["P:2", "P:3", "rows"], writes=["rows"], tag=tag)
        em.op("act", lambda e: e.activation(r1, r2, AF.Ln,
                                            bias=epsb[0:1, 0:1], scale=1.0 / D),
              reads=["rows", "epsb"], writes=["rows"], tag=tag)
        em.op("act", lambda e: e.activation(r2, r1, AF.Exp, scale=-0.5),
              reads=["rows"], writes=["rows"], tag=tag)
        for hf in range(2):
            MM("ps45", 0, 128, hf * 512, hf * 512 + 512, ones1f[:, :],
               rows[0:1, 2 * T + hf * 512:2 * T + hf * 512 + 512], True, True,
               reads=["ones1f", "rows"], tag=tag + "bc")
            MM("ps23", 0, 128, hf * 512, hf * 512 + 512, ones1f[:, :],
               rows[0:1, 0 + hf * 512:0 + hf * 512 + 512], True, True,
               reads=["ones1f", "rows"], tag=tag + "bc")
        em.op("dve", lambda e: e.tensor_copy(rstd_b[:, :], ps45[:, :]),
              reads=["P:4", "P:5"], writes=["rstd_b"], tag=tag)
        em.op("dve", lambda e: e.tensor_copy(mrs_b[:, :], ps23[:, :]),
              reads=["P:2", "P:3"], writes=["mrs_b"], tag=tag)
        for k in range(NK):
            yr = y[:, k * T:(k + 1) * T]
            xr = xn[:, k * T:(k + 1) * T]
            tb = tmpb if k % 2 == 0 else tmpb2
            tbr = "tmpb" if k % 2 == 0 else "tmpb2"
            em.op("dve", lambda e, yr=yr, tb=tb: e.tensor_sub(
                  tb[:, :], yr, mrs_b[:, :]),
                  reads=["y:%d" % k, "mrs_b"], writes=[tbr], tag=tag)
            em.op("dve", lambda e, xr=xr, tb=tb: e.tensor_mul(
                  xr, tb[:, :], rstd_b[:, :]),
                  reads=[tbr, "rstd_b"], writes=["xn:%d" % k], tag=tag)

    def emit_ffn(ci):
        tag = "f%d" % ci
        sem = lambda s: s + str(ci)
        bf = (ci == 2) and ("ffn1" in parts)
        wb = (ci == 1) and ("ffn2" in parts)
        wdbf = bf or (ci == 1)

        em.dma("pool", sem("m_ua"), lambda e: e.dma_start(
            out=ua_sb[:, :], in_=uat), writes=["ua_sb"], tag=tag)
        em.dma("pool", sem("m_ga"), lambda e: e.dma_start(
            out=ga_sb[:, :], in_=gat), writes=["ga_sb"], tag=tag)

        def seg_logits():
            for k in range(NK):
                for hf in range(2):
                    MM("ps45", 0, 8, hf * 512, hf * 512 + 512,
                       gw_sb[:, k * 8:(k + 1) * 8],
                       xn[:, k * T + hf * 512:k * T + hf * 512 + 512],
                       k == 0, k == NK - 1, reads=["gw_sb", "xn:%d" % k], tag=tag + "gl")
            em.op("dve", lambda e: e.tensor_copy(gfm[:, :], ps45[0:8, :]),
                  reads=["P:4", "P:5"], writes=["gfm"], tag=tag)
            for c in range(8):
                TRANS("ps45", 0, 128, c * 8, c * 8 + 8,
                      gfm[0:8, c * 128:(c + 1) * 128], identf[0:8, 0:8],
                      reads=["gfm", "identf"], tag=tag + "glt")
            em.op("dve", lambda e: e.tensor_copy(gt[:, :], ps45[:, 0:64]),
                  reads=["P:4"], writes=["gt"], tag=tag)
        def seg_top2():
            for c in range(8):
                gc = gt[:, c * 8:(c + 1) * 8]
                g2c = g2s[:, 0:8]
                ohc = oh1[:, c * 8:(c + 1) * 8]
                em.op("dve", lambda e, gc=gc, c=c: e.reduce_max(
                      v1a[:, c:c + 1], gc, axis=AX), reads=["gt"], writes=["v1a"], tag=tag)
                em.op("dve", lambda e, gc=gc, ohc=ohc, c=c: e.tensor_scalar(
                      ohc, gc, v1a[:, c:c + 1], None, ALU.is_equal),
                      reads=["gt", "v1a"], writes=["oh1"], tag=tag)
                em.op("dve", lambda e, gc=gc, g2c=g2c, ohc=ohc: e.scalar_tensor_tensor(
                      g2c, ohc, -1e30, gc, ALU.mult, ALU.add),
                      reads=["gt", "oh1"], writes=["g2s"], tag=tag)
                em.op("dve", lambda e, g2c=g2c, c=c: e.reduce_max(
                      v2a[:, c:c + 1], g2c, axis=AX), reads=["g2s"], writes=["v2a"],
                      tag=tag)
                em.op("dve", lambda e, g2c=g2c, c=c: e.tensor_scalar(
                      oh2[:, c * 8:(c + 1) * 8], g2c, v2a[:, c:c + 1], None,
                      ALU.is_equal),
                      reads=["g2s", "v2a"], writes=["oh2"], tag=tag)
        def seg_stats():
            for s in range(2):
                va = v1a if s == 0 else v2a
                vres = "v1a" if s == 0 else "v2a"
                em.op("dve", lambda e, va=va: e.reduce_max(
                      colscr[:, 0:1], va[:, :], axis=AX), reads=[vres],
                      writes=["colscr"], tag=tag)
                TRANS("ps45", 0, 1, 128, 256, colscr[:, 0:1], identf[:, :],
                      reads=["colscr", "identf"], tag=tag)
                em.op("dve", lambda e, s=s: e.reduce_max(
                      stat4[0:1, s:s + 1], ps45[0:1, 128:256], axis=AX),
                      reads=["P:4"], writes=["stat4"], tag=tag)
                em.op("dve", lambda e, s=s: e.tensor_scalar_mul(
                      srow[0:1, 0:1], stat4[0:1, s:s + 1], -1.0),
                      reads=["stat4"], writes=["srow"], tag=tag)
                MM("ps7", 0, 128, 384, 385, ones1f[:, :], srow[0:1, 0:1], True, True,
                   reads=["ones1f", "srow"], tag=tag)
                em.op("dve", lambda e: e.tensor_copy(colscr[:, 2:3], ps7[:, 384:385]),
                      reads=["P:7"], writes=["colscr"], tag=tag)
                em.op("act", lambda e, va=va: e.activation(
                      e_scr[:, :], va[:, :], AF.Exp, bias=colscr[:, 2:3]),
                      reads=[vres, "colscr"], writes=["e_scr"], tag=tag)
                em.op("dve", lambda e: e.reduce_sum(colscr[:, 0:1], e_scr[:, :], axis=AX),
                      reads=["e_scr"], writes=["colscr"], tag=tag)
                TRANS("ps45", 0, 1, 128, 256, colscr[:, 0:1], identf[:, :],
                      reads=["colscr", "identf"], tag=tag)
                em.op("dve", lambda e, s=s: e.reduce_sum(
                      stat4[0:1, 2 + s:3 + s], ps45[0:1, 128:256], axis=AX),
                      reads=["P:4"], writes=["stat4"], tag=tag)
        def seg_cc():
            ccin = cc_bufs["st%di" % ci]
            ccout = cc_bufs["st%do" % ci]
            em.dma("pool", sem("m_sti"), lambda e: e.dma_start(
                out=ccin.ap(), in_=stat4[0:1, :]),
                reads=["stat4"], writes=["d_sti%d" % ci], tag=tag)
            em.cc(sem("cc_st"), lambda e: e.collective_compute(
                  "AllGather", ALU.bypass, replica_groups=GROUPS,
                  ins=[ccin.ap()], outs=[ccout.ap()]),
                  reads=["d_sti%d" % ci], writes=["d_sto%d" % ci], tag=tag)
            em.dma("pool", sem("m_stA"), lambda e: e.dma_start(
                out=rA[0:1, :], in_=ccout.ap()[0:1, :]),
                reads=["d_sto%d" % ci], writes=["rA"], tag=tag)
            em.dma("pool", sem("m_stB"), lambda e: e.dma_start(
                out=rB[0:1, :], in_=ccout.ap()[1:2, :]),
                reads=["d_sto%d" % ci], writes=["rB"], tag=tag)
        def seg_mask():
            for s in range(2):
                ohsb = oh1 if s == 0 else oh2
                ohres = "oh1" if s == 0 else "oh2"
                for c in range(8):
                    TRANS("psT", 0, 8, c * 128, (c + 1) * 128,
                          ohsb[:, c * 8:(c + 1) * 8], identb[:, :],
                          reads=[ohres, "identb"], tag=tag)
                em.op("dve", lambda e, s=s: e.tensor_copy(
                      ohfm[0:8, s * T:(s + 1) * T], psT[0:8, :]),
                      reads=["P:6"], writes=["ohfm"], tag=tag)
                for hf in range(2):
                    MM("ps45", 0, 128, hf * 512, hf * 512 + 512, expb[:, :],
                       ohfm[0:8, s * T + hf * 512:s * T + hf * 512 + 512], True, True,
                       reads=["expb", "ohfm"], tag=tag)
                em.op("dve", lambda e, s=s: e.tensor_copy(
                      mask16[:, s * T:(s + 1) * T], ps45[:, :]),
                      reads=["P:4", "P:5"], writes=["mask16"], tag=tag)
        def seg_Pprod():
            for src_, srcres, dst, dres in ((ua_sb, "ua_sb", p_up, "p_up"),
                                            (ga_sb, "ga_sb", p_gt, "p_gt")):
                for k in range(NK):
                    for hf in range(2):
                        MM("ps45", 0, 128, hf * 512, hf * 512 + 512,
                           src_[:, k * 128:(k + 1) * 128],
                           xn[:, k * T + hf * 512:k * T + hf * 512 + 512],
                           k == 0, k == NK - 1, reads=[srcres, "xn:%d" % k],
                           tag=tag + "P")
                em.op("dve", lambda e, dst=dst: e.tensor_copy(dst[:, :], ps45[:, :]),
                      reads=["P:4", "P:5"], writes=[dres], tag=tag)
        def seg_Pmask():
            for s in range(2):
                for ix, (mat, mres) in enumerate(((p_up, "p_up"), (p_gt, "p_gt"))):
                    dcol = (s * 2 + ix) * T
                    em.op("dve", lambda e, mat=mat, s=s, dcol=dcol: e.tensor_mul(
                          ptm[:, dcol:dcol + T], mat[:, :],
                          mask16[:, s * T:(s + 1) * T]),
                          reads=[mres, "mask16"], writes=["ptm:%d" % (s * 2 + ix)],
                          tag=tag)
        def seg_comb():
            MgR = srow2[0:1, 0:2]
            dAR = srow2[0:1, 2:4]
            dBR = srow2[0:1, 4:6]
            eAR = srow2[0:1, 6:8]
            eBR = srow2[0:1, 8:10]
            tAR = srow2[0:1, 10:12]
            ZgR = srow2[0:1, 12:14]
            cfR = srow2[0:1, 14:16]
            em.op("dve", lambda e: e.tensor_max(MgR, rA[0:1, 0:2], rB[0:1, 0:2]),
                  reads=["rA", "rB"], writes=["srow2"], tag=tag)
            em.op("dve", lambda e: e.tensor_sub(dAR, rA[0:1, 0:2], MgR),
                  reads=["rA", "srow2"], writes=["srow2"], tag=tag)
            em.op("dve", lambda e: e.tensor_sub(dBR, rB[0:1, 0:2], MgR),
                  reads=["rB", "srow2"], writes=["srow2"], tag=tag)
            em.op("act", lambda e: e.activation(eAR, dAR, AF.Exp),
                  reads=["srow2"], writes=["srow2"], tag=tag)
            em.op("act", lambda e: e.activation(eBR, dBR, AF.Exp),
                  reads=["srow2"], writes=["srow2"], tag=tag)
            em.op("dve", lambda e: e.tensor_mul(tAR, eAR, rA[0:1, 2:4]),
                  reads=["rA", "srow2"], writes=["srow2"], tag=tag)
            em.op("dve", lambda e: e.tensor_mul(eBR, eBR, rB[0:1, 2:4]),
                  reads=["rB", "srow2"], writes=["srow2"], tag=tag)
            em.op("dve", lambda e: e.tensor_add(ZgR, tAR, eBR),
                  reads=["srow2"], writes=["srow2"], tag=tag)
            em.op("act", lambda e: e.activation(cfR, ZgR, AF.Ln),
                  reads=["srow2"], writes=["srow2"], tag=tag)
            em.op("dve", lambda e: e.tensor_add(cfR, cfR, MgR),
                  reads=["srow2"], writes=["srow2"], tag=tag)
            em.op("dve", lambda e: e.tensor_scalar_mul(cfR, cfR, -1.0),
                  reads=["srow2"], writes=["srow2"], tag=tag)
            for s in range(2):
                va = v1a if s == 0 else v2a
                vres = "v1a" if s == 0 else "v2a"
                MM("ps7", 0, 128, 384, 385, ones1f[:, :], cfR[0:1, s:s + 1], True, True,
                   reads=["ones1f", "srow2"], tag=tag)
                em.op("dve", lambda e: e.tensor_copy(colscr[:, 2:3], ps7[:, 384:385]),
                      reads=["P:7"], writes=["colscr"], tag=tag)
                em.op("act", lambda e, va=va: e.activation(
                      e_scr[:, :], va[:, :], AF.Exp, bias=colscr[:, 2:3]),
                      reads=[vres, "colscr"], writes=["e_scr"], tag=tag)
                for c in range(8):
                    TRANS("ps45", 0, 1, c * 128, (c + 1) * 128, e_scr[:, c:c + 1],
                          identf[:, :], reads=["e_scr", "identf"], tag=tag)
                em.op("dve", lambda e, s=s: e.tensor_copy(
                      arow[0:1, s * T:(s + 1) * T], ps45[0:1, :]),
                      reads=["P:4", "P:5"], writes=["arow"], tag=tag)
                for hf in range(2):
                    MM("ps45", 0, 128, hf * 512, hf * 512 + 512, ones1f[:, :],
                       arow[0:1, s * T + hf * 512:s * T + hf * 512 + 512], True, True,
                       reads=["ones1f", "arow"], tag=tag)
                em.op("dve", lambda e, s=s: e.tensor_copy(
                      a_b[:, s * T:(s + 1) * T], ps45[:, :]),
                      reads=["P:4", "P:5"], writes=["a_b"], tag=tag)

        WDS = 5
        wd_ctr = [0]

        def wd_dma(gi, d):
            wsl = (wd_ctr[0] + d) % WDS
            wtns = wdslab if wsl < 3 else wdslab2
            wso = (wsl if wsl < 3 else wsl - 3) * GM * 128
            wres = "wdslab:%d" % wsl
            wsrc = wdt_bf if wdbf else wdt
            em.dma("sp" if wdbf else "pool", sem("m_wd%d" % wsl),
                   lambda e, gi=gi, d=d, wso=wso, wtns=wtns, wsrc=wsrc:
                   e.dma_start(
                       out=wtns[:, wso:wso + GM * 128],
                       in_=wsrc[gi, d, :, :, :]),
                   reads=(["fwd:%d:%d" % (gi, d)] if wdbf else []),
                   writes=[wres], tag=tag + "wd")

        def wd_prefetch(gi):
            for d in range(WDS):
                wd_dma(gi, d)

        def wd_pass(gi, with_db, pt0):
            other = "ps23" if pt0 == "ps01" else "ps01"
            for d in range(NK):
                pt_ = pt0 if d % 2 == 0 else other
                tns = PS[pt_]
                pres = pbanks(pt_, 0, 1024)
                wsl = (wd_ctr[0] + d) % WDS
                wtns = wdslab if wsl < 3 else wdslab2
                wso = (wsl if wsl < 3 else wsl - 3) * GM * 128
                wres = "wdslab:%d" % wsl
                for hf in range(2):
                    cs = hf * 512
                    for j in range(GM):
                        MM(pt_, 0, 128, cs, cs + 512,
                           wtns[:, wso + j * 128:wso + (j + 1) * 128],
                           mixed[:, j * T + cs:j * T + cs + 512],
                           j == 0, (j == GM - 1) and not with_db,
                           reads=[wres, "mixed:%d" % j], tag=tag + "wdm")
                    if with_db:
                        MM(pt_, 0, 128, cs, cs + 512,
                           db_sb[:, d * 128:(d + 1) * 128], q_sb[:, cs:cs + 512],
                           False, True, reads=["db_sb", "q_sb"], tag=tag + "dbm")
                    em.op("dve", lambda e, d=d, tns=tns, cs=cs:
                          e.scalar_tensor_tensor(
                              y[:, d * T + cs:d * T + cs + 512],
                              tns[:, cs:cs + 512], 0.5,
                              y[:, d * T + cs:d * T + cs + 512],
                              ALU.mult, ALU.add),
                          reads=[pres[hf], "y:%d" % d],
                          writes=["y:%d" % d], tag=tag + "ya")
                if d + WDS < NK:
                    wd_dma(gi, d + WDS)
            wd_ctr[0] += NK

        NU = 2 * NM

        def PAIR(u):
            return ("ps01", ["P:0", "P:1"]) if u % 2 == 0 else \
                   ("ps23", ["P:2", "P:3"])

        def BUF(u):
            return (bg if u % 2 == 0 else bg2, "bg" if u % 2 == 0 else "bg2",
                    h0 if u % 2 == 0 else h02, "h0" if u % 2 == 0 else "h02")

        def stage_A_base(u):
            m, hf = u // 2, u % 2
            cs = hf * 512
            sl = m % 2
            so = sl * 32 * 128
            slr = "slab:%d" % sl
            ll = m % 4
            lo_ = ll * 3 * 128
            llr = "lslab:%d" % ll
            if hf == 0:
                bsrc = ffnb_bf if bf else ffnb
                lsrc = ffnl_bf if bf else ffnl
                dstream = "sp" if bf else "pool"
                em.dma(dstream, sem("m_slab%d" % sl),
                       lambda e, m=m, so=so, bsrc=bsrc: e.dma_start(
                           out=slab[:, so:so + 32 * 128], in_=bsrc[m, :, :, :]),
                       reads=(["fbm:%d" % m] if bf else []),
                       writes=[slr], tag=tag + "slab")
                em.dma(dstream, sem("m_lslab%d" % ll),
                       lambda e, m=m, lo_=lo_, lsrc=lsrc: e.dma_start(
                           out=lslab[:, lo_:lo_ + 3 * 128], in_=lsrc[m, :, :, :]),
                       reads=(["flm:%d" % m] if bf else []),
                       writes=[llr], tag=tag + "lslab")
                if wb:
                    em.dma("sp", "m_fwb",
                           lambda e, m=m, so=so: e.dma_start(
                               out=ffnb_bf[m, :, :, :],
                               in_=slab[:, so:so + 32 * 128]),
                           reads=[slr], writes=["fbm:%d" % m], tag=tag + "fwb")
                    em.dma("sp", "m_lwb",
                           lambda e, m=m, lo_=lo_: e.dma_start(
                               out=ffnl_bf[m, :, :, :],
                               in_=lslab[:, lo_:lo_ + 3 * 128]),
                           reads=[llr], writes=["flm:%d" % m], tag=tag + "lwb")
                if ci == 1:
                    if m < 22:
                        pcl = (2 * m, 2 * m + 1)
                    elif m < 42:
                        pcl = (44 + (m - 22),)
                    else:
                        pcl = ()
                    for pcidx in pcl:
                        pg, pd = pcidx // NK, pcidx % NK
                        em.dma("pool", "m_pc",
                               lambda e, pg=pg, pd=pd: e.dma_start(
                                   out=wdt_bf[pg, pd, :, :, :],
                                   in_=wdt[pg, pd, :, :, :]),
                               writes=["fwd:%d:%d" % (pg, pd)],
                               tag=tag + "pc")
            pt_, pres = PAIR(u)
            tns = PS[pt_]
            bgb, bgr, _, _ = BUF(u)
            for k in range(NK):
                xr = xn[:, k * T + cs:k * T + cs + 512]
                MM(pt_, 0, 128, 0, 512,
                   slab[:, so + k * 128:so + (k + 1) * 128], xr,
                   k == 0, False, reads=[slr, "xn:%d" % k], tag=tag + "bu")
            for k in range(NK):
                xr = xn[:, k * T + cs:k * T + cs + 512]
                MM(pt_, 0, 128, 512, 1024,
                   slab[:, so + (16 + k) * 128:so + (17 + k) * 128], xr,
                   k == 0, False, reads=[slr, "xn:%d" % k], tag=tag + "bg")
            but = bub[:, (u % 2) * 512:(u % 2) * 512 + 512]
            em.op("act", lambda e, tns=tns, but=but: e.activation(
                  but[:, :], tns[:, 0:512], AF.Copy),
                  reads=[pres[0]], writes=["bub"], tag=tag + "bue")
            em.op("act", lambda e, tns=tns, bgb=bgb: e.activation(
                  bgb[:, :], tns[:, 512:1024], AF.Copy),
                  reads=[pres[1]], writes=[bgr], tag=tag + "bge")
        def stage_A_lo(u):
            m, hf = u // 2, u % 2
            cs = hf * 512
            ll = m % 4
            lo_ = ll * 3 * 128
            llr = "lslab:%d" % ll
            pt_, pres = PAIR(u)
            tns = PS[pt_]
            ubm = lslab[:, lo_:lo_ + 128]
            MM(pt_, 0, 128, 0, 512, ubm, ptm[:, 0 * T + cs:0 * T + cs + 512],
               False, True, reads=[llr, "ptm:0"], tag=tag + "lu0")
            em.op("act", lambda e, tns=tns: e.activation(
                  ss0[:, :], tns[:, 0:512], AF.Silu),
                  reads=[pres[0]], writes=["ss0"], tag=tag + "si0")

        def stage_B1(u):
            m, hf = u // 2, u % 2
            cs = hf * 512
            ll = m % 4
            lo_ = ll * 3 * 128
            llr = "lslab:%d" % ll
            ubm = lslab[:, lo_:lo_ + 128]
            gbm = lslab[:, lo_ + 128:lo_ + 256]
            pt_, pres = PAIR(u)
            tns = PS[pt_]
            MM("ps7", 0, 128, 0, 512, ubm,
               ptm[:, 2 * T + cs:2 * T + cs + 512],
               True, True, reads=[llr, "ptm:2"], tag=tag + "lu1")
            but = bub[:, (u % 2) * 512:(u % 2) * 512 + 512]
            em.op("dve", lambda e, but=but: e.scalar_tensor_tensor(
                  us1[:, :], ps7[:, 0:512], 1.0, but[:, :], ALU.mult, ALU.add),
                  reads=["P:7", "bub"], writes=["us1"], tag=tag + "u1")
            em.op("act", lambda e: e.activation(ss1[:, :], us1[:, :], AF.Silu),
                  reads=["us1"], writes=["ss1"], tag=tag + "si1")
            MM(pt_, 0, 128, 512, 1024, gbm,
               ptm[:, 1 * T + cs:1 * T + cs + 512],
               False, True, reads=[llr, "ptm:1"], tag=tag + "lg0")
            _, _, h0b_, h0r = BUF(u)
            g0t = g0b[:, (u % 2) * 512:(u % 2) * 512 + 512]
            g0r = "g0b"
            em.op("act", lambda e, tns=tns, g0t=g0t: e.activation(
                  g0t[:, :], tns[:, 512:1024], AF.Copy),
                  reads=[pres[1]], writes=[g0r], tag=tag + "g0e")
            em.op("dve", lambda e, g0t=g0t: e.tensor_mul(
                  tsc[:, :], ss0[:, :], g0t[:, :]),
                  reads=["ss0", g0r], writes=["tsc"], tag=tag + "h0a")
            em.op("dve", lambda e, cs=cs, h0b_=h0b_: e.tensor_mul(
                  h0b_[:, :], tsc[:, :], a_b[:, cs:cs + 512]),
                  reads=["tsc", "a_b"], writes=[h0r], tag=tag + "h0b")

        def stage_B2(u):
            m, hf = u // 2, u % 2
            cs = hf * 512
            ll = m % 4
            lo_ = ll * 3 * 128
            llr = "lslab:%d" % ll
            gbm = lslab[:, lo_ + 128:lo_ + 256]
            bgb, bgr, h0b_, h0r = BUF(u)
            MM("ps7", 0, 128, 0, 512, gbm,
               ptm[:, 3 * T + cs:3 * T + cs + 512],
               True, True, reads=[llr, "ptm:3"], tag=tag + "lg1")
            em.op("dve", lambda e, bgb=bgb: e.scalar_tensor_tensor(
                  gs1[:, :], ps7[:, 0:512], 1.0, bgb[:, :], ALU.mult, ALU.add),
                  reads=["P:7", bgr], writes=["gs1"], tag=tag + "g1")
            em.op("dve", lambda e: e.tensor_mul(tsc[:, :], ss1[:, :], gs1[:, :]),
                  reads=["ss1", "gs1"], writes=["tsc"], tag=tag + "h1a")
            em.op("dve", lambda e, cs=cs: e.tensor_mul(
                  h1[:, :], tsc[:, :], a_b[:, T + cs:T + cs + 512]),
                  reads=["tsc", "a_b"], writes=["h1"], tag=tag + "h1b")
            gslot = (u // 2) % GM
            em.op("dve", lambda e, gslot=gslot, cs=cs, h0b_=h0b_: e.tensor_add(
                  mixed[:, gslot * T + cs:gslot * T + cs + 512], h0b_[:, :],
                  h1[:, :]),
                  reads=[h0r, "h1"], writes=["mixed:%d" % gslot], tag=tag + "mx")

        def stage_C_mm(u):
            m, hf = u // 2, u % 2
            ll = m % 4
            lo_ = ll * 3 * 128
            llr = "lslab:%d" % ll
            dam = lslab[:, lo_ + 256:lo_ + 384]
            _, _, h0b_, h0r = BUF(u)
            MM("ps45", 0, 128, 0, 512, dam, h0b_[:, :], True, True,
               reads=[llr, h0r], tag=tag + "q0")
            MM("ps45", 0, 128, 512, 1024, dam, h1[:, :], True, True,
               reads=[llr, "h1"], tag=tag + "q1")

        def stage_C_evac(u):
            m, hf = u // 2, u % 2
            cs = hf * 512
            for s, (c0, bank) in enumerate(((0, "P:4"), (512, "P:5"))):
                qsl = q_acc[:, s * T + cs:s * T + cs + 512]
                if m == 0:
                    em.op("dve", lambda e, qsl=qsl, c0=c0: e.tensor_copy(
                          qsl, ps45[:, c0:c0 + 512]),
                          reads=[bank], writes=["q_acc"], tag=tag + "qe")
                else:
                    em.op("dve", lambda e, qsl=qsl, c0=c0: e.scalar_tensor_tensor(
                          qsl, ps45[:, c0:c0 + 512], 1.0, qsl, ALU.mult, ALU.add),
                          reads=[bank, "q_acc"], writes=["q_acc"], tag=tag + "qe")

        seg_logits()
        stage_A_base(0)
        seg_top2()
        seg_stats()
        seg_cc()
        stage_A_base(1)
        seg_Pprod()
        seg_mask()
        seg_comb()
        seg_Pmask()
        for i in range(NU):
            if i >= 2:
                stage_A_base(i)
            stage_A_lo(i)
            if i >= 2:
                stage_C_mm(i - 2)
            if i >= 1:
                stage_B2(i - 1)
            if i >= 2:
                stage_C_evac(i - 2)
            if (i + 2) % (2 * GM) == 2 * GM - 1 and (i + 2) // (2 * GM) < NGRP:
                wd_prefetch((i + 2) // (2 * GM))
            stage_B1(i)
            if i % (2 * GM) == 0 and i > 0 and i // (2 * GM) <= NGRP - 1:
                fp, _ = PAIR(i - 1)
                wd_pass(i // (2 * GM) - 1, False, fp)
        stage_C_mm(NU - 2)
        stage_C_evac(NU - 2)
        stage_B2(NU - 1)
        stage_C_mm(NU - 1)
        stage_C_evac(NU - 1)
        em.op("dve", lambda e: e.tensor_mul(qt1[:, :], q_acc[:, T:2 * T],
                                            mask16[:, T:2 * T]),
              reads=["q_acc", "mask16"], writes=["qt1"], tag=tag + "qt1")
        em.op("dve", lambda e: e.tensor_mul(q_sb[:, :], q_acc[:, 0:T],
                                            mask16[:, 0:T]),
              reads=["q_acc", "mask16"], writes=["q_sb"], tag=tag + "qs0")
        em.op("dve", lambda e: e.tensor_add(q_sb[:, :], q_sb[:, :], qt1[:, :]),
              reads=["qt1", "q_sb"], writes=["q_sb"], tag=tag + "qs")
        wd_pass(NGRP - 1, True, "ps01")

    def emit_attn():
        tag = "at"
        for h in range(NH):
            sl = h % 2
            so = sl * 32 * 128
            slr = "slab:%d" % sl
            em.dma("pool", "m_kvs%d" % sl,
                   lambda e, h=h, so=so: e.dma_start(
                       out=slab[:, so:so + 32 * 128],
                       in_=kvslab_d[h, :, :, :]),
                   writes=[slr], tag=tag + "kvsl")
            for hf in range(2):
                cs = hf * 512
                for k in range(NK):
                    xr = xn[:, k * T + cs:k * T + cs + 512]
                    MM("ps01", 0, 128, cs, cs + 512,
                       slab[:, so + k * 128:so + (k + 1) * 128], xr,
                       k == 0, k == NK - 1, reads=[slr, "xn:%d" % k], tag=tag + "kp")
                for k in range(NK):
                    xr = xn[:, k * T + cs:k * T + cs + 512]
                    MM("ps23", 0, 128, cs, cs + 512,
                       slab[:, so + (16 + k) * 128:so + (17 + k) * 128], xr,
                       k == 0, k == NK - 1, reads=[slr, "xn:%d" % k], tag=tag + "vp")
            em.op("act", lambda e: e.activation(kfm[:, :], ps01[:, :], AF.Relu,
                                                scale=ISCALE),
                  reads=["P:0", "P:1"], writes=["kfm"], tag=tag + "kr")
            em.op("dve", lambda e: e.tensor_copy(vfm[:, :], ps23[:, :]),
                  reads=["P:2", "P:3"], writes=["vfm"], tag=tag + "ve")
            for c in range(8):
                TRANS("psT", 0, 128, c * 128, (c + 1) * 128,
                      kfm[:, c * 128:(c + 1) * 128], identb[:, :],
                      reads=["kfm", "identb"], tag=tag + "kt")
            em.op("dve", lambda e: e.tensor_copy(ktm[:, :], psT[:, :]),
                  reads=["P:6"], writes=["ktm"], tag=tag)
            for c in range(8):
                TRANS("psT", 0, 128, c * 128, (c + 1) * 128,
                      vfm[:, c * 128:(c + 1) * 128], identb[:, :],
                      reads=["vfm", "identb"], tag=tag + "vt")
            em.op("dve", lambda e: e.tensor_copy(vtm[:, :], psT[:, :]),
                  reads=["P:6"], writes=["vtm"], tag=tag)
            for c in range(8):
                MM("ps45", 0, 128, 0, 128, ktm[:, c * 128:(c + 1) * 128],
                   vtm[:, c * 128:(c + 1) * 128], c == 0, c == 7,
                   reads=["ktm", "vtm"], tag=tag + "kv")
            em.op("dve", lambda e, h=h: e.tensor_copy(
                  kvsb[:, h * 128:(h + 1) * 128], ps45[:, 0:128]),
                  reads=["P:4"], writes=["kvsb"], tag=tag)
        em.dma("pool", "m_kvi", lambda e: e.dma_start(
            out=cc_bufs["kvi"].ap(), in_=kvsb[:, :]),
            reads=["kvsb"], writes=["d_kvi"], tag=tag)
        em.cc("cc_kv", lambda e: e.collective_compute(
              "AllReduce", ALU.add, replica_groups=GROUPS,
              ins=[cc_bufs["kvi"].ap()], outs=[cc_bufs["kvo"].ap()]),
              reads=["d_kvi"], writes=["d_kvo"], tag=tag)

        def q_slab_dma(h):
            sl = h % 2
            so = sl * 32 * 128
            em.dma("pool", "m_qs%d" % sl,
                   lambda e, h=h, so=so: e.dma_start(
                       out=slab[:, so:so + 16 * 128],
                       in_=wqt[h, :, :, :]),
                   writes=["slab:%d" % sl], tag=tag + "qsl")

        def q_proj(hw, h, pt_, pres, skip_dma=False):
            sl = h % 2
            so = sl * 32 * 128
            slr = "slab:%d" % sl
            tns = PS[pt_]
            if not skip_dma:
                q_slab_dma(h)
            for hf in range(2):
                cs = hf * 512
                for k in range(NK):
                    MM(pt_, 0, 128, cs, cs + 512,
                       slab[:, so + k * 128:so + (k + 1) * 128],
                       xn[:, k * T + cs:k * T + cs + 512],
                       k == 0, k == NK - 1, reads=[slr, "xn:%d" % k], tag=tag + "qp")
            em.op("act", lambda e, hw=hw, tns=tns: e.activation(
                  qfm[:, hw * T:(hw + 1) * T], tns[:, :], AF.Relu, scale=ISCALE),
                  reads=pres, writes=["qfm:%d" % hw], tag=tag + "qr")

        def u_proj(hw, h, pt_, pres):
            sl = h % 2
            so = sl * 32 * 128
            slr = "slab:%d" % sl
            tns = PS[pt_]
            em.dma("pool", "m_us%d" % sl,
                   lambda e, h=h, so=so: e.dma_start(
                       out=slab[:, so:so + 16 * 128],
                       in_=wut[h, :, :, :]),
                   writes=[slr], tag=tag + "usl")
            for hf in range(2):
                cs = hf * 512
                for k in range(NK):
                    MM(pt_, 0, 128, cs, cs + 512,
                       slab[:, so + k * 128:so + (k + 1) * 128],
                       xn[:, k * T + cs:k * T + cs + 512],
                       k == 0, k == NK - 1, reads=[slr, "xn:%d" % k], tag=tag + "up")
            em.op("act", lambda e, hw=hw, tns=tns: e.activation(
                  u8[:, hw * T:(hw + 1) * T], tns[:, :], AF.Silu),
                  reads=pres, writes=["u8:%d" % hw], tag=tag + "us")

        def head_o(h):
            hw = h % 8
            ob = obf3[:, (h % 2) * T:(h % 2 + 1) * T]
            obr = "obf:%d" % (h % 2)
            o2t = o2x2[:, (h % 2) * T:(h % 2 + 1) * T]
            o2r = "o2:%d" % (h % 2)
            for hf in range(2):
                cs = hf * 512
                MM("ps45", 0, 128, cs, cs + 512, kvbf[:, h * 128:(h + 1) * 128],
                   qfm[:, hw * T + cs:hw * T + cs + 512], True, True,
                   reads=["kvbf", "qfm:%d" % hw], tag=tag + "o")
            em.op("dve", lambda e, ob=ob: e.tensor_copy(ob[:, :], ps45[:, :]),
                  reads=["P:4", "P:5"], writes=[obr], tag=tag)
            em.op("dve", lambda e, ob=ob, o2t=o2t: e.tensor_mul(
                  o2t[:, :], ob[:, :], ob[:, :]),
                  reads=[obr], writes=[o2r], tag=tag)

        def head_ss(h):
            o2t = o2x2[:, (h % 2) * T:(h % 2 + 1) * T]
            o2r = "o2:%d" % (h % 2)
            for hf in range(2):
                cs = hf * 512
                MM("ps01", 0, 1, cs, cs + 512, ones128b[:, :], o2t[:, cs:cs + 512],
                   True, True, reads=["ones128b", o2r], tag=tag + "ssq")
            em.op("act", lambda e: e.activation(nr[0:1, :], ps01[0:1, :], AF.Ln,
                                                scale=1.0 / 128.0),
                  reads=["P:0", "P:1"], writes=["nr"], tag=tag + "nrm")
            em.op("act", lambda e: e.activation(rrb[0:1, :], nr[0:1, :], AF.Exp,
                                                scale=-0.5),
                  reads=["nr"], writes=["rrb"], tag=tag + "nrm2")
            em.op("dve", lambda e: e.tensor_scalar_min(rrb[0:1, :], rrb[0:1, :],
                                                       1e12),
                  reads=["rrb"], writes=["rrb"], tag=tag)
            for hf in range(2):
                cs = hf * 512
                MM("ps01", 0, 128, cs, cs + 512, ones1b[:, :],
                   rrb[0:1, cs:cs + 512], True, True,
                   reads=["ones1b", "rrb"], tag=tag + "rb")
            em.op("dve", lambda e: e.tensor_copy(rbb[:, :], ps01[:, :]),
                  reads=["P:0", "P:1"], writes=["rbb"], tag=tag)

        def head_ou(h):
            hw = h % 8
            ob = obf3[:, (h % 2) * T:(h % 2 + 1) * T]
            obr = "obf:%d" % (h % 2)
            em.op("dve", lambda e, hw=hw, ob=ob: e.tensor_mul(
                  ob[:, :], ob[:, :], u8[:, hw * T:(hw + 1) * T]),
                  reads=[obr, "u8:%d" % hw], writes=[obr], tag=tag)
            em.op("dve", lambda e, h=h, ob=ob, obr=obr: e.tensor_mul(
                  oub[:, h * T:(h + 1) * T], ob[:, :], rbb[:, :]),
                  reads=[obr, "rbb"], writes=["oub:%d" % h], tag=tag)

        for hw in range(8):
            q_proj(hw, hw, "ps01" if hw % 2 == 0 else "ps23",
                   ["P:0", "P:1"] if hw % 2 == 0 else ["P:2", "P:3"])
        em.dma("pool", "m_kvo", lambda e: e.dma_start(
            out=kvsb[:, :], in_=cc_bufs["kvo"].ap()),
            reads=["d_kvo"], writes=["kvsb"], tag=tag)
        em.op("dve", lambda e: e.tensor_copy(kvbf[:, :], kvsb[:, :]),
              reads=["kvsb"], writes=["kvbf"], tag=tag)
        for wave in range(2):
            hs = list(range(wave * 8, wave * 8 + 8))
            if wave == 1:
                for hw in range(8):
                    q_proj(hw, 8 + hw, "ps01" if hw % 2 == 0 else "ps23",
                           ["P:0", "P:1"] if hw % 2 == 0 else ["P:2", "P:3"])
            for hw in range(8):
                u_proj(hw, hs[hw], "ps01" if hw % 2 == 0 else "ps23",
                       ["P:0", "P:1"] if hw % 2 == 0 else ["P:2", "P:3"])
            for idx, h in enumerate(hs):
                head_o(h)
                if idx >= 1:
                    head_ss(hs[idx - 1])
                    head_ou(hs[idx - 1])
            head_ss(hs[7])
            head_ou(hs[7])

        for d in range(NK):
            sl = d % 2
            so = sl * 32 * 128
            slr = "slab:%d" % sl
            pt_ = "ps01" if d % 2 == 0 else "ps23"
            pres = ["P:0", "P:1"] if d % 2 == 0 else ["P:2", "P:3"]
            tns = PS[pt_]
            em.dma("pool", "m_os%d" % sl,
                   lambda e, d=d, so=so: e.dma_start(
                       out=slab[:, so:so + 16 * 128],
                       in_=wot[d, :, :, :]),
                   writes=[slr], tag=tag + "osl")
            for hf in range(2):
                cs = hf * 512
                for k in range(NK):
                    MM(pt_, 0, 128, cs, cs + 512,
                       slab[:, so + k * 128:so + (k + 1) * 128],
                       oub[:, k * T + cs:k * T + cs + 512],
                       k == 0, k == NK - 1, reads=[slr, "oub:%d" % k],
                       tag=tag + "wo")
            em.op("dve", lambda e, d=d, tns=tns: e.tensor_add(
                  y[:, d * T:(d + 1) * T], y[:, d * T:(d + 1) * T], tns[:, :]),
                  reads=pres + ["y:%d" % d], writes=["y:%d" % d], tag=tag + "ya")

    if "ffn1" in parts:
        emit_ln("ln1")
        emit_ffn(1)
    if "attn" in parts:
        emit_ln("ln2")
        emit_attn()
    if "ffn2" in parts:
        emit_ln("ln3")
        emit_ffn(2)

    em.dma("sp", "m_out", lambda e: e.dma_start(
        out=y_out, in_=y[:, :]),
        reads=YRES, writes=["yout"], tag="store")
    if debug:
        em.dma("sp", "m_out", lambda e: e.dma_start(out=dbg_out, in_=dbg_sb[:, :]),
               reads=["dbg"], writes=["dbgout"], tag="storedbg")

    em.finalize()

    import contextlib
    es = contextlib.ExitStack()
    sems = {}
    for s in Em.STREAMS:
        sems[("s", s)] = es.enter_context(nc.semaphore("sem_" + s))
    for s in em.dma_counts:
        sems[("d", s)] = es.enter_context(nc.semaphore("dma_" + s))
    for s in em.cc_counts:
        sems[("cc", s)] = es.enter_context(nc.semaphore("ccs_" + s))

    eng_of = {"pe": "tensor", "act": "scalar", "dve": "vector", "pool": "gpsimd",
              "sp": "sync"}

    with nc.Block() as block:
        def make_body(sname):
            def body(eng):
                for o in em.ops[sname]:
                    for (k, v) in em.waits[o.gidx]:
                        val = 16 * v if k[0] == "d" else v
                        eng.wait_ge(sems[(k[0], k[1])], val)
                    inst = o.fn(eng)
                    if o.kind == "d":
                        inst.then_inc(sems[("d", o.dma_sem)], 16)
                    elif o.kind == "cc":
                        inst.then_inc(sems[("cc", o.dma_sem)], 1)
                    elif o.marked:
                        inst.then_inc(sems[("s", sname)], 1)
                if sname == "sp":
                    eng.wait_ge(sems[("d", "m_out")], 16 * em.dma_counts["m_out"])
            return body

        for sname in Em.STREAMS:
            getattr(block, eng_of[sname])(make_body(sname))

    es.close()
    return nc, sorted(d_in.keys())


def host_inputs(inp, parts=ALL_PARTS):
    f = np.float32
    w = {}

    def tile4(WT):
        K = WT.shape[0] // 128
        M = WT.shape[1] // 128
        return np.ascontiguousarray(WT.reshape(K, 128, M, 128).transpose(2, 0, 1, 3))

    if "attn" in parts:
        wq = np.asarray(inp["L0_attn_wq"])
        wk = np.asarray(inp["L0_attn_wk"])
        wv = np.asarray(inp["L0_attn_wv"])
        wu = np.asarray(inp["L0_attn_wu"])
        wo = np.asarray(inp["L0_attn_wo"])
        pm = lambda a: np.ascontiguousarray(a.transpose(0, 2, 1, 3))
        w["wqt"] = pm(tile4(np.ascontiguousarray(wq.T)))
        w["wut"] = pm(tile4(np.ascontiguousarray(wu.T)))
        w["wot"] = pm(tile4(np.ascontiguousarray(wo.T)))
        kt = tile4(np.ascontiguousarray(wk.T))
        vt = tile4(np.ascontiguousarray(wv.T))
        w["kvslab"] = pm(np.ascontiguousarray(np.concatenate([kt, vt], axis=1)))
    w["gwt"] = np.ascontiguousarray(
        np.asarray(inp["L0_ffn_gate_w"]).T.reshape(NK, 128, 8).transpose(1, 0, 2))
    w["uat"] = np.ascontiguousarray(
        np.asarray(inp["L0_ffn_up_A"]).transpose(2, 0, 1).reshape(NK, 128, ER)
        .transpose(1, 0, 2))
    w["gat"] = np.ascontiguousarray(
        np.asarray(inp["L0_ffn_gate_A"]).transpose(2, 0, 1).reshape(NK, 128, ER)
        .transpose(1, 0, 2))
    wupT = np.asarray(inp["L0_ffn_w_up"]).T
    wgtT = np.asarray(inp["L0_ffn_w_gate"]).T
    UB = np.asarray(inp["L0_ffn_up_B"]).transpose(0, 2, 1).reshape(ER, H)
    GB = np.asarray(inp["L0_ffn_gate_B"]).transpose(0, 2, 1).reshape(ER, H)
    DA = np.asarray(inp["L0_ffn_down_A"]).transpose(2, 0, 1).reshape(H, ER)
    ffnm = np.empty((NM, 35, 128, 128), f)
    for m in range(NM):
        for k in range(NK):
            ffnm[m, k] = wupT[k * 128:(k + 1) * 128, m * 128:(m + 1) * 128]
            ffnm[m, 16 + k] = wgtT[k * 128:(k + 1) * 128, m * 128:(m + 1) * 128]
        ffnm[m, 32] = UB[:, m * 128:(m + 1) * 128]
        ffnm[m, 33] = GB[:, m * 128:(m + 1) * 128]
        ffnm[m, 34] = DA[m * 128:(m + 1) * 128, :]
    w["ffnb"] = np.ascontiguousarray(ffnm[:, :32].transpose(0, 2, 1, 3))
    w["ffnl"] = np.ascontiguousarray(ffnm[:, 32:].transpose(0, 2, 1, 3))
    wdT = np.asarray(inp["L0_ffn_w_down"]).T
    wdtl = np.empty((NGRP, NK, GM, 128, 128), f)
    for g in range(NGRP):
        for d in range(NK):
            for j in range(GM):
                m = g * GM + j
                wdtl[g, d, j] = wdT[m * 128:(m + 1) * 128, d * 128:(d + 1) * 128]
    w["wdt"] = np.ascontiguousarray(wdtl.transpose(0, 1, 3, 2, 4))
    DB = np.asarray(inp["L0_ffn_down_B"]).transpose(0, 2, 1).reshape(ER, D)
    w["dbt"] = np.ascontiguousarray(DB.reshape(128, NK, 128))
    w["c_ident"] = np.eye(128, dtype=f)
    exp = np.zeros((8, 128), f)
    for e in range(8):
        exp[e, e * 16:(e + 1) * 16] = 2.0
    w["c_expand"] = exp
    return w


def run_parts(inputs, parts, debug=False):
    nc, names = build_nc(parts, debug=debug)
    shared = host_inputs(inputs, parts)
    x = np.asarray(inputs["L0_attn_x"])
    in_maps = []
    for c in range(NCORES):
        b, hl = c // 2, c % 2
        im = dict(shared)
        xt = x[b, hl * T:(hl + 1) * T, :].T.astype(np.float32)
        im["x_fm"] = np.ascontiguousarray(
            xt.reshape(NK, 128, T).transpose(1, 0, 2).reshape(128, NK * T))
        in_maps.append(im)
    res = run_bass_kernel_spmd(nc, in_maps, core_ids=list(range(NCORES)))
    out = np.empty((B, L, D), np.float32)
    for c in range(NCORES):
        b, hl = c // 2, c % 2
        ypm = res.results[c]["y_out"].reshape(128, NK, T)
        out[b, hl * T:(hl + 1) * T, :] = ypm.transpose(2, 1, 0).reshape(T, D)
    if debug:
        global DBG
        DBG = {"c%d" % c: res.results[c]["dbg_out"] for c in range(NCORES)}
    return out


DBG = None


def kernel(**inputs):
    return run_parts(inputs, ALL_PARTS)


# version 20
# speedup vs baseline: 1.0070x; 1.0070x over previous
import numpy as np
import concourse.bass as bass
import concourse.mybir as mybir
from concourse.bass_utils import run_bass_kernel_spmd

F32 = mybir.dt.float32
BF16 = mybir.dt.bfloat16
U8 = mybir.dt.uint8
AF = mybir.ActivationFunctionType
ALU = mybir.AluOpType
AX = mybir.AxisListType.X

NCORES = 8
B, L, D = 4, 2048, 2048
T = 1024
NK = D // 128
H = 5632
NM = H // 128
NH = 16
E, R = 8, 16
ER = E * R
ISCALE = float(1.0 / np.sqrt(128.0))
EPS = 1e-5
GROUPS = [[0, 1], [2, 3], [4, 5], [6, 7]]
NGRP = 4
GM = NM // NGRP

ALL_PARTS = ("ffn1", "attn", "ffn2")


class Op:
    __slots__ = ("stream", "fn", "deps", "selfdeps", "gidx", "marked", "kind",
                 "dma_sem", "dma_cnt", "tag", "mcount", "sidx")

    def __init__(self, stream, fn, kind="c", tag=""):
        self.stream = stream
        self.fn = fn
        self.deps = []
        self.selfdeps = []
        self.marked = False
        self.kind = kind
        self.dma_sem = None
        self.dma_cnt = 0
        self.tag = tag


class Em:
    STREAMS = ("pe", "act", "dve", "pool", "sp")

    def __init__(self):
        self.ops = {s: [] for s in self.STREAMS}
        self.glist = []
        self.res = {}
        self.dma_counts = {}
        self.cc_counts = {}
        self.resmap = {}

    def _expand(self, names):
        out = []
        for n in names:
            out.extend(self.resmap.get(n, (n,)))
        return out

    def _res(self, name):
        st = self.res.get(name)
        if st is None:
            st = {"writers": {}, "readers": {}, "xcl": name.startswith("P:")}
            self.res[name] = st
        return st

    def _add(self, op, reads, writes):
        op.gidx = len(self.glist)
        op.sidx = len(self.ops[op.stream])
        self.glist.append(op)
        self.ops[op.stream].append(op)
        dep = {}

        selfdep = {}

        def add_dep(o, selfok=False):
            if o is op:
                return
            if o.kind == "c" and o.stream == op.stream:
                if selfok and op.stream != "pe" and op.kind == "c":
                    selfdep[id(o)] = o
                return
            dep[id(o)] = o

        for r in reads:
            st = self._res(r)
            if st["xcl"]:
                for o in list(st["writers"].values()) + list(st["readers"].values()):
                    add_dep(o, selfok=True)
            else:
                for o in st["writers"].values():
                    add_dep(o, selfok=True)
        for w in writes:
            st = self._res(w)
            for o in st["writers"].values():
                add_dep(o, selfok=True)
            for o in st["readers"].values():
                add_dep(o, selfok=True)
        op.deps = list(dep.values())
        op.selfdeps = list(selfdep.values())
        skey = op.stream if op.kind == "c" else ("%s#%s" % (op.kind, op.dma_sem))
        for r in reads:
            self._res(r)["readers"][skey] = op
        for w in writes:
            st = self._res(w)
            st["writers"] = {skey: op}
            st["readers"] = {}
        return op

    def op(self, stream, fn, reads=(), writes=(), tag=""):
        return self._add(Op(stream, fn, "c", tag), self._expand(reads),
                         self._expand(writes))

    def dma(self, stream, sem, fn, reads=(), writes=(), tag=""):
        o = Op(stream, fn, "d", tag)
        o.dma_sem = sem
        self.dma_counts[sem] = self.dma_counts.get(sem, 0) + 1
        o.dma_cnt = self.dma_counts[sem]
        return self._add(o, self._expand(reads), self._expand(writes))

    def cc(self, sem, fn, reads=(), writes=(), tag=""):
        o = Op("pool", fn, "cc", tag)
        o.dma_sem = sem
        self.cc_counts[sem] = self.cc_counts.get(sem, 0) + 1
        o.dma_cnt = self.cc_counts[sem]
        return self._add(o, self._expand(reads), self._expand(writes))

    SELFDIST = 8

    def finalize(self):
        for o in self.glist:
            for d in o.deps:
                d.marked = True
            for d in o.selfdeps:
                if o.sidx - d.sidx <= self.SELFDIST:
                    d.marked = True
        for s in self.STREAMS:
            c = 0
            for o in self.ops[s]:
                if o.kind == "c" and o.marked:
                    c += 1
                o.mcount = c

        def ckey(o):
            return ("s", o.stream) if o.kind == "c" else (o.kind, o.dma_sem)

        def cval(o):
            return o.mcount if o.kind == "c" else o.dma_cnt

        clocks = [None] * len(self.glist)
        prev = {s: None for s in self.STREAMS}
        self.waits = [[] for _ in self.glist]
        for o in self.glist:
            clk = {}
            p = prev[o.stream]
            if p is not None:
                clk.update(clocks[p.gidx])
            best = {}
            for d in o.deps:
                k, v = ckey(d), cval(d)
                if clk.get(k, 0) >= v:
                    continue
                cur = best.get(k)
                if cur is None or v > cur[1]:
                    best[k] = (d, v)
            selfv = 0
            for d in o.selfdeps:
                if o.sidx - d.sidx <= self.SELFDIST:
                    selfv = max(selfv, d.mcount)
            if selfv > 0:
                self.waits[o.gidx].append((("s", o.stream), selfv))
            for k, (d, v) in best.items():
                self.waits[o.gidx].append((k, v))
                for kk, vv in clocks[d.gidx].items():
                    if clk.get(kk, 0) < vv:
                        clk[kk] = vv
                if clk.get(k, 0) < v:
                    clk[k] = v
            if o.kind == "c" and o.marked:
                k, v = ckey(o), cval(o)
                if clk.get(k, 0) < v:
                    clk[k] = v
            clocks[o.gidx] = clk
            prev[o.stream] = o


def build_nc(parts=ALL_PARTS, debug=False):
    nc = bass.Bass("TRN2", target_bir_lowering=False)
    d_in = {}

    def din(name, shape):
        d_in[name] = nc.dram_tensor(name, list(shape), F32, kind="ExternalInput").ap()
        return d_in[name]

    x_fm = din("x_fm", (128, NK * T))
    gwt = din("gwt", (128, NK, 8))
    uat = din("uat", (128, NK, ER))
    gat = din("gat", (128, NK, ER))
    ffnb = din("ffnb", (NM, 128, 32, 128))
    ffnl = din("ffnl", (NM, 128, 3, 128))
    wdt = din("wdt", (NGRP, NK, 128, GM, 128))
    dbt = din("dbt", (128, NK, 128))
    c_ident = din("c_ident", (128, 128))
    c_expand = din("c_expand", (8, 128))
    if "attn" in parts:
        wqt = din("wqt", (NK, 128, NK, 128))
        wut = din("wut", (NK, 128, NK, 128))
        wot = din("wot", (NK, 128, NK, 128))
        kvslab_d = din("kvslab", (NH, 128, 32, 128))

    y_out = nc.dram_tensor("y_out", [128, NK * T], F32, kind="ExternalOutput").ap()
    dbg_out = nc.dram_tensor("dbg_out", [128, 64], F32, kind="ExternalOutput").ap() if debug else None

    cc_bufs = {}
    for nm, shape in [("st1i", (1, 4)), ("st1o", (2, 4)),
                      ("st2i", (1, 4)), ("st2o", (2, 4)),
                      ("kvi", (128, 2048)), ("kvo", (128, 2048))]:
        cc_bufs[nm] = nc.dram_tensor("cc_" + nm, list(shape), F32, kind="Internal")

    ffnb_bf = nc.dram_tensor("ffnb_bf", [NM, 128, 32, 128], BF16,
                             kind="Internal").ap()
    ffnl_bf = nc.dram_tensor("ffnl_bf", [NM, 128, 3, 128], BF16,
                             kind="Internal").ap()
    wdt_bf = nc.dram_tensor("wdt_bf", [NGRP, NK, 128, GM, 128], BF16,
                            kind="Internal").ap()

    em = Em()

    def S(name, shape, dtype):
        return nc.alloc_sbuf_tensor(name, list(shape), dtype)

    y = S("y", (128, NK * T), F32)
    xn = S("xn", (128, NK * T), BF16)
    slab = S("slab", (128, 2 * 32 * 128), BF16)
    lslab = S("lslab", (128, 4 * 3 * 128), BF16)
    db_sb = S("db_sb", (128, NK * 128), BF16)
    identf = S("identf", (128, 128), F32)
    identb = S("identb", (128, 128), BF16)
    expb = S("expb", (8, 128), BF16)
    ones1f = S("ones1f", (1, 128), F32)
    ones1b = S("ones1b", (1, 128), BF16)
    ones128b = S("ones128b", (128, 1), BF16)
    gw_sb = S("gw_sb", (128, 128), BF16)
    g2s = S("g2s", (128, 8), F32)
    v1a = S("v1a", (128, 8), F32)
    v2a = S("v2a", (128, 8), F32)
    oh1 = S("oh1", (128, 64), BF16)
    oh2 = S("oh2", (128, 64), BF16)
    e_scr = S("e_scr", (128, 8), F32)
    colscr = S("colscr", (128, 4), F32)
    stat4 = S("stat4", (1, 4), F32)
    rA = S("rA", (1, 4), F32)
    rB = S("rB", (1, 4), F32)
    dbg_sb = S("dbg_sb", (128, 64), F32) if debug else None
    epsb = S("epsb", (128, 1), F32)

    CH = 2048
    ARENA_BYTES = 88320
    arena = nc.alloc_sbuf_tensor("arena", [128, ARENA_BYTES], U8)
    abase = nc.lookup_mloc(arena).addr

    def A(name, shape, dtype, off, reskey=None):
        per = int(np.prod(shape[1:])) * mybir.dt.size(dtype)
        assert off % 32 == 0 and off + per <= ARENA_BYTES, (name, off, per)
        t = nc.alloc_sbuf_tensor_at(name, list(shape), dtype, offset=abase + off)
        c0, c1 = off // CH, (off + per - 1) // CH
        em.resmap[reskey or name] = tuple("AR:%d" % c for c in range(c0, c1 + 1))
        return t

    mixed = A("mixed", (128, GM * T), BF16, 0, reskey="_mixedfull")
    for j in range(GM):
        em.resmap["mixed:%d" % j] = ("AR:%d" % j,)
    tmpa = A("tmpa", (128, T), F32, 0)
    tmpb = A("tmpb", (128, T), BF16, 0)
    tmpb2 = A("tmpb2", (128, T), BF16, 2048)
    gfm = A("gfm", (8, T), F32, 4096)
    ua_sb = A("ua_sb", (128, 2048), BF16, 8192)
    ga_sb = A("ga_sb", (128, 2048), BF16, 12288)
    ohfm = A("ohfm", (8, 2 * T), BF16, 16384)
    ptm = A("ptm", (128, 4 * T), BF16, 22528, reskey="_ptmfull")
    for i in range(4):
        em.resmap["ptm:%d" % i] = ("AR:%d" % (11 + i),)
    a_b = A("a_b", (128, 2 * T), BF16, 30720)
    mask16 = A("mask16", (128, 2 * T), BF16, 34816)
    rows = A("rows", (1, 3 * T), F32, 38912)
    arow = A("arow", (1, 2 * T), F32, 38912, reskey="arow")
    rstd_b = A("rstd_b", (128, T), F32, 51200)
    mrs_b = A("mrs_b", (128, T), F32, 55296)
    bub = A("bub", (128, 2 * 512), BF16, 59392)
    bg = A("bg", (128, 512), F32, 61440)
    us1 = A("us1", (128, 512), BF16, 63488)
    gs1 = A("gs1", (128, 512), BF16, 65536)
    ss0 = A("ss0", (128, 512), BF16, 67584)
    ss1 = A("ss1", (128, 512), BF16, 69632)
    tsc = A("tsc", (128, 512), BF16, 71680)
    ybf = A("ybf", (128, 2 * T), BF16, 59392, reskey="ybf")
    ysq = A("ysq", (128, 2 * T), BF16, 63488, reskey="ysq")
    bg2 = A("bg2", (128, 512), F32, 38912)
    h02 = A("h02", (128, 512), BF16, 40960)
    q_acc = A("q_acc", (128, 2 * T), F32, 43008)
    g0b = A("g0b", (128, 2 * 512), BF16, 57344)
    wdslab2 = A("wdslab2", (128, 2 * GM * 128), BF16, 51200, reskey="_wdslab2full")
    em.resmap["wdslab:3"] = ("AR:25", "AR:26")
    em.resmap["wdslab:4"] = ("AR:26", "AR:27")
    srow = A("srow", (1, 16), F32, 56832)
    srow2 = A("srow2", (1, 16), F32, 56896)
    gt = A("gt", (128, 64), F32, 36864)
    h0 = A("h0", (128, 512), BF16, 77824)
    h1 = A("h1", (128, 512), BF16, 78848)
    wdslab = A("wdslab", (128, 3 * GM * 128), BF16, 79872, reskey="_wdslabfull")
    em.resmap["wdslab:0"] = ("AR:39", "AR:40")
    em.resmap["wdslab:1"] = ("AR:40", "AR:41")
    em.resmap["wdslab:2"] = ("AR:41", "AR:42", "AR:43")
    p_up = A("p_up", (128, T), BF16, 73728)
    p_gt = A("p_gt", (128, T), BF16, 75776)
    qt1 = A("qt1", (128, T), BF16, 73728, reskey="qt1")
    q_sb = A("q_sb", (128, T), BF16, 75776, reskey="q_sb")
    if "attn" in parts:
        oub = A("oub", (128, NK * T), BF16, 0, reskey="_oubfull")
        for hh in range(NK):
            em.resmap["oub:%d" % hh] = ("AR:%d" % hh,)
        kvbf = A("kvbf", (128, NH * 128), BF16, 32768)

        qfm = A("qfm", (128, 8 * T), BF16, 38912, reskey="_qfmfull")
        for hw in range(8):
            em.resmap["qfm:%d" % hw] = ("AR:%d" % (19 + hw),)
        kfm = A("kfm", (128, T), BF16, 55296)
        vfm = A("vfm", (128, T), BF16, 57344)
        ktm = A("ktm", (128, T), BF16, 59392)
        vtm = A("vtm", (128, T), BF16, 61440)
        kvsb = A("kvsb", (128, NH * 128), F32, 63488)
        u8 = A("u8", (128, 8 * T), BF16, 55296, reskey="_u8full")
        for hw in range(8):
            em.resmap["u8:%d" % hw] = ("AR:%d" % (27 + hw),)
        obf3 = A("obf3", (128, 2 * T), BF16, 71680, reskey="_obf3")
        em.resmap["obf:0"] = ("AR:35",)
        em.resmap["obf:1"] = ("AR:36",)
        rbb = A("rbb", (128, T), BF16, 75776)
        o2x2 = A("o2x2", (128, 2 * T), BF16, 77824, reskey="_o2x2")
        em.resmap["o2:0"] = ("AR:38",)
        em.resmap["o2:1"] = ("AR:39",)
        nr = A("nr", (1, T), F32, 81920)
        rrb = A("rrb", (1, T), BF16, 86016)

    ps01 = nc.alloc_psum_tensor("ps01", [128, 1024], F32)
    ps23 = nc.alloc_psum_tensor("ps23", [128, 1024], F32)
    ps45 = nc.alloc_psum_tensor("ps45", [128, 1024], F32)
    psT = nc.alloc_psum_tensor("psT", [128, 1024], BF16)
    ps7 = nc.alloc_psum_tensor("ps7", [128, 512], F32)
    PS = {"ps01": ps01, "ps23": ps23, "ps45": ps45, "psT": psT, "ps7": ps7}
    BANKBASE = {"ps01": 0, "ps23": 2, "ps45": 4, "psT": 6, "ps7": 7}

    def pbanks(t, c0, c1):
        bw = 1024 if t == "psT" else 512
        return ["P:%d" % (BANKBASE[t] + b) for b in range(c0 // bw, (c1 - 1) // bw + 1)]

    def MM(pt_, p0, p1, c0, c1, lhsT, rhs, start, stop, reads, tag=""):
        tns = PS[pt_]
        em.op("pe",
              lambda e, tns=tns, p0=p0, p1=p1, c0=c0, c1=c1, lhsT=lhsT, rhs=rhs,
              start=start, stop=stop:
              e.matmul(tns[p0:p1, c0:c1], lhsT, rhs, start=start, stop=stop),
              reads=reads, writes=pbanks(pt_, c0, c1), tag=tag)

    def TRANS(pt_, p0, p1, c0, c1, in_ap, ident, reads, tag=""):
        tns = PS[pt_]
        em.op("pe",
              lambda e, tns=tns, p0=p0, p1=p1, c0=c0, c1=c1, in_ap=in_ap, ident=ident:
              e.transpose(tns[p0:p1, c0:c1], in_ap, ident),
              reads=reads, writes=pbanks(pt_, c0, c1), tag=tag)

    YRES = ["y:%d" % k for k in range(NK)]

    em.dma("sp", "m_x", lambda e: e.dma_start(
        out=y[:, :], in_=x_fm),
        writes=YRES, tag="loadx")
    em.dma("sp", "m_c", lambda e: e.dma_start(out=identf[:, :], in_=c_ident),
           writes=["identf"])
    em.dma("pool", "m_c2", lambda e: e.dma_start(out=identb[:, :], in_=c_ident),
           writes=["identb"])
    em.dma("pool", "m_c3", lambda e: e.dma_start(out=expb[:, :], in_=c_expand),
           writes=["expb"])
    em.op("dve", lambda e: e.memset(ones1f[:, :], 1.0), writes=["ones1f"])
    em.op("dve", lambda e: e.memset(ones1b[:, :], 1.0), writes=["ones1b"])
    em.op("dve", lambda e: e.memset(epsb[:, :], EPS), writes=["epsb"])
    em.op("dve", lambda e: e.memset(ones128b[:, :], 1.0), writes=["ones128b"])
    em.dma("pool", "m_gw", lambda e: e.dma_start(
        out=gw_sb[:, :], in_=gwt), writes=["gw_sb"])
    em.dma("pool", "m_db", lambda e: e.dma_start(
        out=db_sb[:, :], in_=dbt), writes=["db_sb"])

    def emit_ln(tag):
        for k in range(NK):
            yr = y[:, k * T:(k + 1) * T]
            so_ = (k % 2) * T
            em.op("dve", lambda e, yr=yr, so_=so_: e.tensor_copy(
                  ybf[:, so_:so_ + T], yr),
                  reads=["y:%d" % k], writes=["ybf"], tag=tag)
            em.op("act", lambda e, yr=yr, so_=so_: e.activation(
                  ysq[:, so_:so_ + T], yr, AF.Square),
                  reads=["y:%d" % k], writes=["ysq"], tag=tag)
            for hf in range(2):
                MM("ps01", 0, 1, hf * 512, hf * 512 + 512, ones128b[:, :],
                   ybf[:, so_ + hf * 512:so_ + hf * 512 + 512], k == 0, k == NK - 1,
                   reads=["ones128b", "ybf"], tag=tag + "sum")
                MM("ps23", 0, 1, hf * 512, hf * 512 + 512, ones128b[:, :],
                   ysq[:, so_ + hf * 512:so_ + hf * 512 + 512], k == 0, k == NK - 1,
                   reads=["ones128b", "ysq"], tag=tag + "sq")
        r0 = rows[0:1, 0:T]
        r1 = rows[0:1, T:2 * T]
        r2 = rows[0:1, 2 * T:3 * T]
        em.op("dve", lambda e: e.tensor_scalar_mul(r0, ps01[0:1, :], 1.0 / D),
              reads=["P:0", "P:1"], writes=["rows"], tag=tag)
        em.op("dve", lambda e: e.tensor_mul(r2, r0, ps01[0:1, :]),
              reads=["P:0", "P:1", "rows"], writes=["rows"], tag=tag)
        em.op("dve", lambda e: e.tensor_sub(r2, ps23[0:1, :], r2),
              reads=["P:2", "P:3", "rows"], writes=["rows"], tag=tag)
        em.op("act", lambda e: e.activation(r1, r2, AF.Ln,
                                            bias=epsb[0:1, 0:1], scale=1.0 / D),
              reads=["rows", "epsb"], writes=["rows"], tag=tag)
        em.op("act", lambda e: e.activation(r2, r1, AF.Exp, scale=-0.5),
              reads=["rows"], writes=["rows"], tag=tag)
        for hf in range(2):
            MM("ps45", 0, 128, hf * 512, hf * 512 + 512, ones1f[:, :],
               rows[0:1, 2 * T + hf * 512:2 * T + hf * 512 + 512], True, True,
               reads=["ones1f", "rows"], tag=tag + "bc")
            MM("ps23", 0, 128, hf * 512, hf * 512 + 512, ones1f[:, :],
               rows[0:1, 0 + hf * 512:0 + hf * 512 + 512], True, True,
               reads=["ones1f", "rows"], tag=tag + "bc")
        em.op("dve", lambda e: e.tensor_copy(rstd_b[:, :], ps45[:, :]),
              reads=["P:4", "P:5"], writes=["rstd_b"], tag=tag)
        em.op("dve", lambda e: e.tensor_copy(mrs_b[:, :], ps23[:, :]),
              reads=["P:2", "P:3"], writes=["mrs_b"], tag=tag)
        for k in range(NK):
            yr = y[:, k * T:(k + 1) * T]
            xr = xn[:, k * T:(k + 1) * T]
            tb = tmpb if k % 2 == 0 else tmpb2
            tbr = "tmpb" if k % 2 == 0 else "tmpb2"
            em.op("dve", lambda e, yr=yr, tb=tb: e.tensor_sub(
                  tb[:, :], yr, mrs_b[:, :]),
                  reads=["y:%d" % k, "mrs_b"], writes=[tbr], tag=tag)
            em.op("dve", lambda e, xr=xr, tb=tb: e.tensor_mul(
                  xr, tb[:, :], rstd_b[:, :]),
                  reads=[tbr, "rstd_b"], writes=["xn:%d" % k], tag=tag)

    def emit_ffn(ci):
        tag = "f%d" % ci
        sem = lambda s: s + str(ci)
        bf = (ci == 2) and ("ffn1" in parts)
        wb = (ci == 1) and ("ffn2" in parts)
        wdbf = bf or (ci == 1)

        em.dma("pool", sem("m_ua"), lambda e: e.dma_start(
            out=ua_sb[:, :], in_=uat), writes=["ua_sb"], tag=tag)
        em.dma("pool", sem("m_ga"), lambda e: e.dma_start(
            out=ga_sb[:, :], in_=gat), writes=["ga_sb"], tag=tag)

        def seg_logits():
            for k in range(NK):
                for hf in range(2):
                    MM("ps45", 0, 8, hf * 512, hf * 512 + 512,
                       gw_sb[:, k * 8:(k + 1) * 8],
                       xn[:, k * T + hf * 512:k * T + hf * 512 + 512],
                       k == 0, k == NK - 1, reads=["gw_sb", "xn:%d" % k], tag=tag + "gl")
            em.op("dve", lambda e: e.tensor_copy(gfm[:, :], ps45[0:8, :]),
                  reads=["P:4", "P:5"], writes=["gfm"], tag=tag)
            for c in range(8):
                TRANS("ps45", 0, 128, c * 8, c * 8 + 8,
                      gfm[0:8, c * 128:(c + 1) * 128], identf[0:8, 0:8],
                      reads=["gfm", "identf"], tag=tag + "glt")
            em.op("dve", lambda e: e.tensor_copy(gt[:, :], ps45[:, 0:64]),
                  reads=["P:4"], writes=["gt"], tag=tag)
        def seg_top2():
            for c in range(8):
                gc = gt[:, c * 8:(c + 1) * 8]
                g2c = g2s[:, 0:8]
                ohc = oh1[:, c * 8:(c + 1) * 8]
                em.op("dve", lambda e, gc=gc, c=c: e.reduce_max(
                      v1a[:, c:c + 1], gc, axis=AX), reads=["gt"], writes=["v1a"], tag=tag)
                em.op("dve", lambda e, gc=gc, ohc=ohc, c=c: e.tensor_scalar(
                      ohc, gc, v1a[:, c:c + 1], None, ALU.is_equal),
                      reads=["gt", "v1a"], writes=["oh1"], tag=tag)
                em.op("dve", lambda e, gc=gc, g2c=g2c, ohc=ohc: e.scalar_tensor_tensor(
                      g2c, ohc, -1e30, gc, ALU.mult, ALU.add),
                      reads=["gt", "oh1"], writes=["g2s"], tag=tag)
                em.op("dve", lambda e, g2c=g2c, c=c: e.reduce_max(
                      v2a[:, c:c + 1], g2c, axis=AX), reads=["g2s"], writes=["v2a"],
                      tag=tag)
                em.op("dve", lambda e, g2c=g2c, c=c: e.tensor_scalar(
                      oh2[:, c * 8:(c + 1) * 8], g2c, v2a[:, c:c + 1], None,
                      ALU.is_equal),
                      reads=["g2s", "v2a"], writes=["oh2"], tag=tag)
        def seg_stats():
            for s in range(2):
                va = v1a if s == 0 else v2a
                vres = "v1a" if s == 0 else "v2a"
                em.op("dve", lambda e, va=va: e.reduce_max(
                      colscr[:, 0:1], va[:, :], axis=AX), reads=[vres],
                      writes=["colscr"], tag=tag)
                TRANS("ps45", 0, 1, 128, 256, colscr[:, 0:1], identf[:, :],
                      reads=["colscr", "identf"], tag=tag)
                em.op("dve", lambda e, s=s: e.reduce_max(
                      stat4[0:1, s:s + 1], ps45[0:1, 128:256], axis=AX),
                      reads=["P:4"], writes=["stat4"], tag=tag)
                em.op("dve", lambda e, s=s: e.tensor_scalar_mul(
                      srow[0:1, 0:1], stat4[0:1, s:s + 1], -1.0),
                      reads=["stat4"], writes=["srow"], tag=tag)
                MM("ps7", 0, 128, 384, 385, ones1f[:, :], srow[0:1, 0:1], True, True,
                   reads=["ones1f", "srow"], tag=tag)
                em.op("dve", lambda e: e.tensor_copy(colscr[:, 2:3], ps7[:, 384:385]),
                      reads=["P:7"], writes=["colscr"], tag=tag)
                em.op("act", lambda e, va=va: e.activation(
                      e_scr[:, :], va[:, :], AF.Exp, bias=colscr[:, 2:3]),
                      reads=[vres, "colscr"], writes=["e_scr"], tag=tag)
                em.op("dve", lambda e: e.reduce_sum(colscr[:, 0:1], e_scr[:, :], axis=AX),
                      reads=["e_scr"], writes=["colscr"], tag=tag)
                TRANS("ps45", 0, 1, 128, 256, colscr[:, 0:1], identf[:, :],
                      reads=["colscr", "identf"], tag=tag)
                em.op("dve", lambda e, s=s: e.reduce_sum(
                      stat4[0:1, 2 + s:3 + s], ps45[0:1, 128:256], axis=AX),
                      reads=["P:4"], writes=["stat4"], tag=tag)
        def seg_cc():
            ccin = cc_bufs["st%di" % ci]
            ccout = cc_bufs["st%do" % ci]
            em.dma("pool", sem("m_sti"), lambda e: e.dma_start(
                out=ccin.ap(), in_=stat4[0:1, :]),
                reads=["stat4"], writes=["d_sti%d" % ci], tag=tag)
            em.cc(sem("cc_st"), lambda e: e.collective_compute(
                  "AllGather", ALU.bypass, replica_groups=GROUPS,
                  ins=[ccin.ap()], outs=[ccout.ap()]),
                  reads=["d_sti%d" % ci], writes=["d_sto%d" % ci], tag=tag)
            em.dma("pool", sem("m_stA"), lambda e: e.dma_start(
                out=rA[0:1, :], in_=ccout.ap()[0:1, :]),
                reads=["d_sto%d" % ci], writes=["rA"], tag=tag)
            em.dma("pool", sem("m_stB"), lambda e: e.dma_start(
                out=rB[0:1, :], in_=ccout.ap()[1:2, :]),
                reads=["d_sto%d" % ci], writes=["rB"], tag=tag)
        def seg_mask():
            for s in range(2):
                ohsb = oh1 if s == 0 else oh2
                ohres = "oh1" if s == 0 else "oh2"
                for c in range(8):
                    TRANS("psT", 0, 8, c * 128, (c + 1) * 128,
                          ohsb[:, c * 8:(c + 1) * 8], identb[:, :],
                          reads=[ohres, "identb"], tag=tag)
                em.op("dve", lambda e, s=s: e.tensor_copy(
                      ohfm[0:8, s * T:(s + 1) * T], psT[0:8, :]),
                      reads=["P:6"], writes=["ohfm"], tag=tag)
                for hf in range(2):
                    MM("ps45", 0, 128, hf * 512, hf * 512 + 512, expb[:, :],
                       ohfm[0:8, s * T + hf * 512:s * T + hf * 512 + 512], True, True,
                       reads=["expb", "ohfm"], tag=tag)
                em.op("dve", lambda e, s=s: e.tensor_copy(
                      mask16[:, s * T:(s + 1) * T], ps45[:, :]),
                      reads=["P:4", "P:5"], writes=["mask16"], tag=tag)
        def seg_Pprod():
            for src_, srcres, dst, dres in ((ua_sb, "ua_sb", p_up, "p_up"),
                                            (ga_sb, "ga_sb", p_gt, "p_gt")):
                for k in range(NK):
                    for hf in range(2):
                        MM("ps45", 0, 128, hf * 512, hf * 512 + 512,
                           src_[:, k * 128:(k + 1) * 128],
                           xn[:, k * T + hf * 512:k * T + hf * 512 + 512],
                           k == 0, k == NK - 1, reads=[srcres, "xn:%d" % k],
                           tag=tag + "P")
                em.op("dve", lambda e, dst=dst: e.tensor_copy(dst[:, :], ps45[:, :]),
                      reads=["P:4", "P:5"], writes=[dres], tag=tag)
        def seg_Pmask():
            for s in range(2):
                for ix, (mat, mres) in enumerate(((p_up, "p_up"), (p_gt, "p_gt"))):
                    dcol = (s * 2 + ix) * T
                    em.op("dve", lambda e, mat=mat, s=s, dcol=dcol: e.tensor_mul(
                          ptm[:, dcol:dcol + T], mat[:, :],
                          mask16[:, s * T:(s + 1) * T]),
                          reads=[mres, "mask16"], writes=["ptm:%d" % (s * 2 + ix)],
                          tag=tag)
        def seg_comb():
            MgR = srow2[0:1, 0:2]
            dAR = srow2[0:1, 2:4]
            dBR = srow2[0:1, 4:6]
            eAR = srow2[0:1, 6:8]
            eBR = srow2[0:1, 8:10]
            tAR = srow2[0:1, 10:12]
            ZgR = srow2[0:1, 12:14]
            cfR = srow2[0:1, 14:16]
            em.op("dve", lambda e: e.tensor_max(MgR, rA[0:1, 0:2], rB[0:1, 0:2]),
                  reads=["rA", "rB"], writes=["srow2"], tag=tag)
            em.op("dve", lambda e: e.tensor_sub(dAR, rA[0:1, 0:2], MgR),
                  reads=["rA", "srow2"], writes=["srow2"], tag=tag)
            em.op("dve", lambda e: e.tensor_sub(dBR, rB[0:1, 0:2], MgR),
                  reads=["rB", "srow2"], writes=["srow2"], tag=tag)
            em.op("act", lambda e: e.activation(eAR, dAR, AF.Exp),
                  reads=["srow2"], writes=["srow2"], tag=tag)
            em.op("act", lambda e: e.activation(eBR, dBR, AF.Exp),
                  reads=["srow2"], writes=["srow2"], tag=tag)
            em.op("dve", lambda e: e.tensor_mul(tAR, eAR, rA[0:1, 2:4]),
                  reads=["rA", "srow2"], writes=["srow2"], tag=tag)
            em.op("dve", lambda e: e.tensor_mul(eBR, eBR, rB[0:1, 2:4]),
                  reads=["rB", "srow2"], writes=["srow2"], tag=tag)
            em.op("dve", lambda e: e.tensor_add(ZgR, tAR, eBR),
                  reads=["srow2"], writes=["srow2"], tag=tag)
            em.op("act", lambda e: e.activation(cfR, ZgR, AF.Ln),
                  reads=["srow2"], writes=["srow2"], tag=tag)
            em.op("dve", lambda e: e.tensor_add(cfR, cfR, MgR),
                  reads=["srow2"], writes=["srow2"], tag=tag)
            em.op("dve", lambda e: e.tensor_scalar_mul(cfR, cfR, -1.0),
                  reads=["srow2"], writes=["srow2"], tag=tag)
            for s in range(2):
                va = v1a if s == 0 else v2a
                vres = "v1a" if s == 0 else "v2a"
                MM("ps7", 0, 128, 384, 385, ones1f[:, :], cfR[0:1, s:s + 1], True, True,
                   reads=["ones1f", "srow2"], tag=tag)
                em.op("dve", lambda e: e.tensor_copy(colscr[:, 2:3], ps7[:, 384:385]),
                      reads=["P:7"], writes=["colscr"], tag=tag)
                em.op("act", lambda e, va=va: e.activation(
                      e_scr[:, :], va[:, :], AF.Exp, bias=colscr[:, 2:3]),
                      reads=[vres, "colscr"], writes=["e_scr"], tag=tag)
                for c in range(8):
                    TRANS("ps45", 0, 1, c * 128, (c + 1) * 128, e_scr[:, c:c + 1],
                          identf[:, :], reads=["e_scr", "identf"], tag=tag)
                em.op("dve", lambda e, s=s: e.tensor_copy(
                      arow[0:1, s * T:(s + 1) * T], ps45[0:1, :]),
                      reads=["P:4", "P:5"], writes=["arow"], tag=tag)
                for hf in range(2):
                    MM("ps45", 0, 128, hf * 512, hf * 512 + 512, ones1f[:, :],
                       arow[0:1, s * T + hf * 512:s * T + hf * 512 + 512], True, True,
                       reads=["ones1f", "arow"], tag=tag)
                em.op("dve", lambda e, s=s: e.tensor_copy(
                      a_b[:, s * T:(s + 1) * T], ps45[:, :]),
                      reads=["P:4", "P:5"], writes=["a_b"], tag=tag)

        WDS = 5
        wd_ctr = [0]

        def wd_dma(gi, d):
            wsl = (wd_ctr[0] + d) % WDS
            wtns = wdslab if wsl < 3 else wdslab2
            wso = (wsl if wsl < 3 else wsl - 3) * GM * 128
            wres = "wdslab:%d" % wsl
            wsrc = wdt_bf if wdbf else wdt
            em.dma("sp" if wdbf else "pool", sem("m_wd%d" % wsl),
                   lambda e, gi=gi, d=d, wso=wso, wtns=wtns, wsrc=wsrc:
                   e.dma_start(
                       out=wtns[:, wso:wso + GM * 128],
                       in_=wsrc[gi, d, :, :, :]),
                   reads=(["fwd:%d:%d" % (gi, d)] if wdbf else []),
                   writes=[wres], tag=tag + "wd")

        def wd_prefetch(gi):
            for d in range(WDS):
                wd_dma(gi, d)

        def wd_pass(gi, with_db, pt0):
            other = "ps23" if pt0 == "ps01" else "ps01"
            for d in range(NK):
                pt_ = pt0 if d % 2 == 0 else other
                tns = PS[pt_]
                pres = pbanks(pt_, 0, 1024)
                wsl = (wd_ctr[0] + d) % WDS
                wtns = wdslab if wsl < 3 else wdslab2
                wso = (wsl if wsl < 3 else wsl - 3) * GM * 128
                wres = "wdslab:%d" % wsl
                for hf in range(2):
                    cs = hf * 512
                    for j in range(GM):
                        MM(pt_, 0, 128, cs, cs + 512,
                           wtns[:, wso + j * 128:wso + (j + 1) * 128],
                           mixed[:, j * T + cs:j * T + cs + 512],
                           j == 0, (j == GM - 1) and not with_db,
                           reads=[wres, "mixed:%d" % j], tag=tag + "wdm")
                    if with_db:
                        MM(pt_, 0, 128, cs, cs + 512,
                           db_sb[:, d * 128:(d + 1) * 128], q_sb[:, cs:cs + 512],
                           False, True, reads=["db_sb", "q_sb"], tag=tag + "dbm")
                    em.op("dve", lambda e, d=d, tns=tns, cs=cs:
                          e.scalar_tensor_tensor(
                              y[:, d * T + cs:d * T + cs + 512],
                              tns[:, cs:cs + 512], 0.5,
                              y[:, d * T + cs:d * T + cs + 512],
                              ALU.mult, ALU.add),
                          reads=[pres[hf], "y:%d" % d],
                          writes=["y:%d" % d], tag=tag + "ya")
                if d + WDS < NK:
                    wd_dma(gi, d + WDS)
            wd_ctr[0] += NK

        NU = 2 * NM

        def PAIR(u):
            return ("ps01", ["P:0", "P:1"]) if u % 2 == 0 else \
                   ("ps23", ["P:2", "P:3"])

        def BUF(u):
            return (bg if u % 2 == 0 else bg2, "bg" if u % 2 == 0 else "bg2",
                    h0 if u % 2 == 0 else h02, "h0" if u % 2 == 0 else "h02")

        def stage_A_base(u):
            m, hf = u // 2, u % 2
            cs = hf * 512
            sl = m % 2
            so = sl * 32 * 128
            slr = "slab:%d" % sl
            ll = m % 4
            lo_ = ll * 3 * 128
            llr = "lslab:%d" % ll
            if hf == 0:
                bsrc = ffnb_bf if bf else ffnb
                lsrc = ffnl_bf if bf else ffnl
                dstream = "sp" if bf else "pool"
                em.dma(dstream, sem("m_slab%d" % sl),
                       lambda e, m=m, so=so, bsrc=bsrc: e.dma_start(
                           out=slab[:, so:so + 32 * 128], in_=bsrc[m, :, :, :]),
                       reads=(["fbm:%d" % m] if bf else []),
                       writes=[slr], tag=tag + "slab")
                em.dma(dstream, sem("m_lslab%d" % ll),
                       lambda e, m=m, lo_=lo_, lsrc=lsrc: e.dma_start(
                           out=lslab[:, lo_:lo_ + 3 * 128], in_=lsrc[m, :, :, :]),
                       reads=(["flm:%d" % m] if bf else []),
                       writes=[llr], tag=tag + "lslab")
                if wb:
                    em.dma("sp", "m_fwb",
                           lambda e, m=m, so=so: e.dma_start(
                               out=ffnb_bf[m, :, :, :],
                               in_=slab[:, so:so + 32 * 128]),
                           reads=[slr], writes=["fbm:%d" % m], tag=tag + "fwb")
                    em.dma("sp", "m_lwb",
                           lambda e, m=m, lo_=lo_: e.dma_start(
                               out=ffnl_bf[m, :, :, :],
                               in_=lslab[:, lo_:lo_ + 3 * 128]),
                           reads=[llr], writes=["flm:%d" % m], tag=tag + "lwb")
                if ci == 1:
                    if m < 22:
                        pcl = (2 * m, 2 * m + 1)
                    elif m < 42:
                        pcl = (44 + (m - 22),)
                    else:
                        pcl = ()
                    for pcidx in pcl:
                        pg, pd = pcidx // NK, pcidx % NK
                        em.dma("pool", "m_pc",
                               lambda e, pg=pg, pd=pd: e.dma_start(
                                   out=wdt_bf[pg, pd, :, :, :],
                                   in_=wdt[pg, pd, :, :, :]),
                               writes=["fwd:%d:%d" % (pg, pd)],
                               tag=tag + "pc")
            pt_, pres = PAIR(u)
            tns = PS[pt_]
            bgb, bgr, _, _ = BUF(u)
            for k in range(NK):
                xr = xn[:, k * T + cs:k * T + cs + 512]
                MM(pt_, 0, 128, 0, 512,
                   slab[:, so + k * 128:so + (k + 1) * 128], xr,
                   k == 0, False, reads=[slr, "xn:%d" % k], tag=tag + "bu")
            for k in range(NK):
                xr = xn[:, k * T + cs:k * T + cs + 512]
                MM(pt_, 0, 128, 512, 1024,
                   slab[:, so + (16 + k) * 128:so + (17 + k) * 128], xr,
                   k == 0, False, reads=[slr, "xn:%d" % k], tag=tag + "bg")
            but = bub[:, (u % 2) * 512:(u % 2) * 512 + 512]
            em.op("act", lambda e, tns=tns, but=but: e.activation(
                  but[:, :], tns[:, 0:512], AF.Copy),
                  reads=[pres[0]], writes=["bub"], tag=tag + "bue")
            em.op("act", lambda e, tns=tns, bgb=bgb: e.activation(
                  bgb[:, :], tns[:, 512:1024], AF.Copy),
                  reads=[pres[1]], writes=[bgr], tag=tag + "bge")
        def stage_A_lo(u):
            m, hf = u // 2, u % 2
            cs = hf * 512
            ll = m % 4
            lo_ = ll * 3 * 128
            llr = "lslab:%d" % ll
            pt_, pres = PAIR(u)
            tns = PS[pt_]
            ubm = lslab[:, lo_:lo_ + 128]
            MM(pt_, 0, 128, 0, 512, ubm, ptm[:, 0 * T + cs:0 * T + cs + 512],
               False, True, reads=[llr, "ptm:0"], tag=tag + "lu0")
            em.op("act", lambda e, tns=tns: e.activation(
                  ss0[:, :], tns[:, 0:512], AF.Silu),
                  reads=[pres[0]], writes=["ss0"], tag=tag + "si0")

        def stage_B1(u):
            m, hf = u // 2, u % 2
            cs = hf * 512
            ll = m % 4
            lo_ = ll * 3 * 128
            llr = "lslab:%d" % ll
            ubm = lslab[:, lo_:lo_ + 128]
            gbm = lslab[:, lo_ + 128:lo_ + 256]
            pt_, pres = PAIR(u)
            tns = PS[pt_]
            MM("ps7", 0, 128, 0, 512, ubm,
               ptm[:, 2 * T + cs:2 * T + cs + 512],
               True, True, reads=[llr, "ptm:2"], tag=tag + "lu1")
            but = bub[:, (u % 2) * 512:(u % 2) * 512 + 512]
            em.op("dve", lambda e, but=but: e.scalar_tensor_tensor(
                  us1[:, :], ps7[:, 0:512], 1.0, but[:, :], ALU.mult, ALU.add),
                  reads=["P:7", "bub"], writes=["us1"], tag=tag + "u1")
            em.op("act", lambda e: e.activation(ss1[:, :], us1[:, :], AF.Silu),
                  reads=["us1"], writes=["ss1"], tag=tag + "si1")
            MM(pt_, 0, 128, 512, 1024, gbm,
               ptm[:, 1 * T + cs:1 * T + cs + 512],
               False, True, reads=[llr, "ptm:1"], tag=tag + "lg0")
            _, _, h0b_, h0r = BUF(u)
            g0t = g0b[:, (u % 2) * 512:(u % 2) * 512 + 512]
            g0r = "g0b"
            em.op("act", lambda e, tns=tns, g0t=g0t: e.activation(
                  g0t[:, :], tns[:, 512:1024], AF.Copy),
                  reads=[pres[1]], writes=[g0r], tag=tag + "g0e")
            em.op("dve", lambda e, g0t=g0t: e.tensor_mul(
                  tsc[:, :], ss0[:, :], g0t[:, :]),
                  reads=["ss0", g0r], writes=["tsc"], tag=tag + "h0a")
            em.op("dve", lambda e, cs=cs, h0b_=h0b_: e.tensor_mul(
                  h0b_[:, :], tsc[:, :], a_b[:, cs:cs + 512]),
                  reads=["tsc", "a_b"], writes=[h0r], tag=tag + "h0b")

        def stage_B2(u):
            m, hf = u // 2, u % 2
            cs = hf * 512
            ll = m % 4
            lo_ = ll * 3 * 128
            llr = "lslab:%d" % ll
            gbm = lslab[:, lo_ + 128:lo_ + 256]
            bgb, bgr, h0b_, h0r = BUF(u)
            MM("ps7", 0, 128, 0, 512, gbm,
               ptm[:, 3 * T + cs:3 * T + cs + 512],
               True, True, reads=[llr, "ptm:3"], tag=tag + "lg1")
            em.op("dve", lambda e, bgb=bgb: e.scalar_tensor_tensor(
                  gs1[:, :], ps7[:, 0:512], 1.0, bgb[:, :], ALU.mult, ALU.add),
                  reads=["P:7", bgr], writes=["gs1"], tag=tag + "g1")
            em.op("dve", lambda e: e.tensor_mul(tsc[:, :], ss1[:, :], gs1[:, :]),
                  reads=["ss1", "gs1"], writes=["tsc"], tag=tag + "h1a")
            em.op("dve", lambda e, cs=cs: e.tensor_mul(
                  h1[:, :], tsc[:, :], a_b[:, T + cs:T + cs + 512]),
                  reads=["tsc", "a_b"], writes=["h1"], tag=tag + "h1b")
            gslot = (u // 2) % GM
            em.op("dve", lambda e, gslot=gslot, cs=cs, h0b_=h0b_: e.tensor_add(
                  mixed[:, gslot * T + cs:gslot * T + cs + 512], h0b_[:, :],
                  h1[:, :]),
                  reads=[h0r, "h1"], writes=["mixed:%d" % gslot], tag=tag + "mx")

        def stage_C_mm(u):
            m, hf = u // 2, u % 2
            ll = m % 4
            lo_ = ll * 3 * 128
            llr = "lslab:%d" % ll
            dam = lslab[:, lo_ + 256:lo_ + 384]
            _, _, h0b_, h0r = BUF(u)
            MM("ps45", 0, 128, 0, 512, dam, h0b_[:, :], True, True,
               reads=[llr, h0r], tag=tag + "q0")
            MM("ps45", 0, 128, 512, 1024, dam, h1[:, :], True, True,
               reads=[llr, "h1"], tag=tag + "q1")

        def stage_C_evac(u):
            m, hf = u // 2, u % 2
            cs = hf * 512
            for s, (c0, bank) in enumerate(((0, "P:4"), (512, "P:5"))):
                qsl = q_acc[:, s * T + cs:s * T + cs + 512]
                if m == 0:
                    em.op("dve", lambda e, qsl=qsl, c0=c0: e.tensor_copy(
                          qsl, ps45[:, c0:c0 + 512]),
                          reads=[bank], writes=["q_acc"], tag=tag + "qe")
                else:
                    em.op("dve", lambda e, qsl=qsl, c0=c0: e.scalar_tensor_tensor(
                          qsl, ps45[:, c0:c0 + 512], 1.0, qsl, ALU.mult, ALU.add),
                          reads=[bank, "q_acc"], writes=["q_acc"], tag=tag + "qe")

        seg_logits()
        stage_A_base(0)
        seg_top2()
        seg_stats()
        seg_cc()
        stage_A_base(1)
        seg_Pprod()
        seg_mask()
        seg_comb()
        seg_Pmask()
        for i in range(NU):
            if i >= 2:
                stage_A_base(i)
            stage_A_lo(i)
            if i >= 2:
                stage_C_mm(i - 2)
            if i >= 1:
                stage_B2(i - 1)
            if i >= 2:
                stage_C_evac(i - 2)
            if (i + 2) % (2 * GM) == 2 * GM - 1 and (i + 2) // (2 * GM) < NGRP:
                wd_prefetch((i + 2) // (2 * GM))
            stage_B1(i)
            if i % (2 * GM) == 0 and i > 0 and i // (2 * GM) <= NGRP - 1:
                fp, _ = PAIR(i - 1)
                wd_pass(i // (2 * GM) - 1, False, fp)
        stage_C_mm(NU - 2)
        stage_C_evac(NU - 2)
        stage_B2(NU - 1)
        stage_C_mm(NU - 1)
        stage_C_evac(NU - 1)
        em.op("dve", lambda e: e.tensor_mul(qt1[:, :], q_acc[:, T:2 * T],
                                            mask16[:, T:2 * T]),
              reads=["q_acc", "mask16"], writes=["qt1"], tag=tag + "qt1")
        em.op("dve", lambda e: e.tensor_mul(q_sb[:, :], q_acc[:, 0:T],
                                            mask16[:, 0:T]),
              reads=["q_acc", "mask16"], writes=["q_sb"], tag=tag + "qs0")
        em.op("dve", lambda e: e.tensor_add(q_sb[:, :], q_sb[:, :], qt1[:, :]),
              reads=["qt1", "q_sb"], writes=["q_sb"], tag=tag + "qs")
        wd_pass(NGRP - 1, True, "ps01")

    def emit_attn():
        tag = "at"
        em.resmap["qw:0"] = ("AR:39", "AR:40")
        em.resmap["qw:1"] = ("AR:41", "AR:42")
        for qh in range(2):
            em.dma("pool", "m_qw%d" % qh,
                   lambda e, qh=qh: e.dma_start(
                       out=wdslab[:, qh * 16 * 128:(qh + 1) * 16 * 128],
                       in_=wqt[qh, :, :, :]),
                   writes=["qw:%d" % qh], tag=tag + "qsl")
        for h in range(NH):
            sl = h % 2
            so = sl * 32 * 128
            slr = "slab:%d" % sl
            em.dma("pool", "m_kvs%d" % sl,
                   lambda e, h=h, so=so: e.dma_start(
                       out=slab[:, so:so + 32 * 128],
                       in_=kvslab_d[h, :, :, :]),
                   writes=[slr], tag=tag + "kvsl")
            for hf in range(2):
                cs = hf * 512
                for k in range(NK):
                    xr = xn[:, k * T + cs:k * T + cs + 512]
                    MM("ps01", 0, 128, cs, cs + 512,
                       slab[:, so + k * 128:so + (k + 1) * 128], xr,
                       k == 0, k == NK - 1, reads=[slr, "xn:%d" % k], tag=tag + "kp")
                for k in range(NK):
                    xr = xn[:, k * T + cs:k * T + cs + 512]
                    MM("ps23", 0, 128, cs, cs + 512,
                       slab[:, so + (16 + k) * 128:so + (17 + k) * 128], xr,
                       k == 0, k == NK - 1, reads=[slr, "xn:%d" % k], tag=tag + "vp")
            em.op("act", lambda e: e.activation(kfm[:, :], ps01[:, :], AF.Relu,
                                                scale=ISCALE),
                  reads=["P:0", "P:1"], writes=["kfm"], tag=tag + "kr")
            em.op("dve", lambda e: e.tensor_copy(vfm[:, :], ps23[:, :]),
                  reads=["P:2", "P:3"], writes=["vfm"], tag=tag + "ve")
            for c in range(8):
                TRANS("psT", 0, 128, c * 128, (c + 1) * 128,
                      kfm[:, c * 128:(c + 1) * 128], identb[:, :],
                      reads=["kfm", "identb"], tag=tag + "kt")
            em.op("dve", lambda e: e.tensor_copy(ktm[:, :], psT[:, :]),
                  reads=["P:6"], writes=["ktm"], tag=tag)
            for c in range(8):
                TRANS("psT", 0, 128, c * 128, (c + 1) * 128,
                      vfm[:, c * 128:(c + 1) * 128], identb[:, :],
                      reads=["vfm", "identb"], tag=tag + "vt")
            em.op("dve", lambda e: e.tensor_copy(vtm[:, :], psT[:, :]),
                  reads=["P:6"], writes=["vtm"], tag=tag)
            for c in range(8):
                MM("ps45", 0, 128, 0, 128, ktm[:, c * 128:(c + 1) * 128],
                   vtm[:, c * 128:(c + 1) * 128], c == 0, c == 7,
                   reads=["ktm", "vtm"], tag=tag + "kv")
            em.op("dve", lambda e, h=h: e.tensor_copy(
                  kvsb[:, h * 128:(h + 1) * 128], ps45[:, 0:128]),
                  reads=["P:4"], writes=["kvsb"], tag=tag)
        em.dma("pool", "m_kvi", lambda e: e.dma_start(
            out=cc_bufs["kvi"].ap(), in_=kvsb[:, :]),
            reads=["kvsb"], writes=["d_kvi"], tag=tag)
        em.cc("cc_kv", lambda e: e.collective_compute(
              "AllReduce", ALU.add, replica_groups=GROUPS,
              ins=[cc_bufs["kvi"].ap()], outs=[cc_bufs["kvo"].ap()]),
              reads=["d_kvi"], writes=["d_kvo"], tag=tag)

        def q_loc(h):
            if h < 2:
                return wdslab, h * 16 * 128, "qw:%d" % h, "m_qw%d" % h
            sl = h % 2
            return slab, sl * 32 * 128, "slab:%d" % sl, "m_qs%d" % sl

        def q_slab_dma(h):
            qt, so, slr, smn = q_loc(h)
            em.dma("pool", smn,
                   lambda e, h=h, qt=qt, so=so: e.dma_start(
                       out=qt[:, so:so + 16 * 128],
                       in_=wqt[h, :, :, :]),
                   writes=[slr], tag=tag + "qsl")

        def q_proj(hw, h, pt_, pres, skip_dma=False):
            slab_, so, slr, _ = q_loc(h)
            tns = PS[pt_]
            if not skip_dma:
                q_slab_dma(h)
            for hf in range(2):
                cs = hf * 512
                for k in range(NK):
                    MM(pt_, 0, 128, cs, cs + 512,
                       slab_[:, so + k * 128:so + (k + 1) * 128],
                       xn[:, k * T + cs:k * T + cs + 512],
                       k == 0, k == NK - 1, reads=[slr, "xn:%d" % k], tag=tag + "qp")
            em.op("act", lambda e, hw=hw, tns=tns: e.activation(
                  qfm[:, hw * T:(hw + 1) * T], tns[:, :], AF.Relu, scale=ISCALE),
                  reads=pres, writes=["qfm:%d" % hw], tag=tag + "qr")

        def u_proj(hw, h, pt_, pres):
            sl = h % 2
            so = sl * 32 * 128
            slr = "slab:%d" % sl
            tns = PS[pt_]
            em.dma("pool", "m_us%d" % sl,
                   lambda e, h=h, so=so: e.dma_start(
                       out=slab[:, so:so + 16 * 128],
                       in_=wut[h, :, :, :]),
                   writes=[slr], tag=tag + "usl")
            for hf in range(2):
                cs = hf * 512
                for k in range(NK):
                    MM(pt_, 0, 128, cs, cs + 512,
                       slab[:, so + k * 128:so + (k + 1) * 128],
                       xn[:, k * T + cs:k * T + cs + 512],
                       k == 0, k == NK - 1, reads=[slr, "xn:%d" % k], tag=tag + "up")
            em.op("act", lambda e, hw=hw, tns=tns: e.activation(
                  u8[:, hw * T:(hw + 1) * T], tns[:, :], AF.Silu),
                  reads=pres, writes=["u8:%d" % hw], tag=tag + "us")

        def head_o(h):
            hw = h % 8
            ob = obf3[:, (h % 2) * T:(h % 2 + 1) * T]
            obr = "obf:%d" % (h % 2)
            o2t = o2x2[:, (h % 2) * T:(h % 2 + 1) * T]
            o2r = "o2:%d" % (h % 2)
            for hf in range(2):
                cs = hf * 512
                MM("ps45", 0, 128, cs, cs + 512, kvbf[:, h * 128:(h + 1) * 128],
                   qfm[:, hw * T + cs:hw * T + cs + 512], True, True,
                   reads=["kvbf", "qfm:%d" % hw], tag=tag + "o")
            em.op("dve", lambda e, ob=ob: e.tensor_copy(ob[:, :], ps45[:, :]),
                  reads=["P:4", "P:5"], writes=[obr], tag=tag)
            em.op("dve", lambda e, ob=ob, o2t=o2t: e.tensor_mul(
                  o2t[:, :], ob[:, :], ob[:, :]),
                  reads=[obr], writes=[o2r], tag=tag)

        def head_ss(h):
            o2t = o2x2[:, (h % 2) * T:(h % 2 + 1) * T]
            o2r = "o2:%d" % (h % 2)
            for hf in range(2):
                cs = hf * 512
                MM("ps01", 0, 1, cs, cs + 512, ones128b[:, :], o2t[:, cs:cs + 512],
                   True, True, reads=["ones128b", o2r], tag=tag + "ssq")
            em.op("act", lambda e: e.activation(nr[0:1, :], ps01[0:1, :], AF.Ln,
                                                scale=1.0 / 128.0),
                  reads=["P:0", "P:1"], writes=["nr"], tag=tag + "nrm")
            em.op("act", lambda e: e.activation(rrb[0:1, :], nr[0:1, :], AF.Exp,
                                                scale=-0.5),
                  reads=["nr"], writes=["rrb"], tag=tag + "nrm2")
            em.op("dve", lambda e: e.tensor_scalar_min(rrb[0:1, :], rrb[0:1, :],
                                                       1e12),
                  reads=["rrb"], writes=["rrb"], tag=tag)
            for hf in range(2):
                cs = hf * 512
                MM("ps01", 0, 128, cs, cs + 512, ones1b[:, :],
                   rrb[0:1, cs:cs + 512], True, True,
                   reads=["ones1b", "rrb"], tag=tag + "rb")
            em.op("dve", lambda e: e.tensor_copy(rbb[:, :], ps01[:, :]),
                  reads=["P:0", "P:1"], writes=["rbb"], tag=tag)

        def head_ou(h):
            hw = h % 8
            ob = obf3[:, (h % 2) * T:(h % 2 + 1) * T]
            obr = "obf:%d" % (h % 2)
            em.op("dve", lambda e, hw=hw, ob=ob: e.tensor_mul(
                  ob[:, :], ob[:, :], u8[:, hw * T:(hw + 1) * T]),
                  reads=[obr, "u8:%d" % hw], writes=[obr], tag=tag)
            em.op("dve", lambda e, h=h, ob=ob, obr=obr: e.tensor_mul(
                  oub[:, h * T:(h + 1) * T], ob[:, :], rbb[:, :]),
                  reads=[obr, "rbb"], writes=["oub:%d" % h], tag=tag)

        for hw in range(8):
            q_proj(hw, hw, "ps01" if hw % 2 == 0 else "ps23",
                   ["P:0", "P:1"] if hw % 2 == 0 else ["P:2", "P:3"],
                   skip_dma=(hw < 2))
        em.dma("pool", "m_kvo", lambda e: e.dma_start(
            out=kvsb[:, :], in_=cc_bufs["kvo"].ap()),
            reads=["d_kvo"], writes=["kvsb"], tag=tag)
        em.op("dve", lambda e: e.tensor_copy(kvbf[:, :], kvsb[:, :]),
              reads=["kvsb"], writes=["kvbf"], tag=tag)
        for wave in range(2):
            hs = list(range(wave * 8, wave * 8 + 8))
            if wave == 1:
                for hw in range(8):
                    q_proj(hw, 8 + hw, "ps01" if hw % 2 == 0 else "ps23",
                           ["P:0", "P:1"] if hw % 2 == 0 else ["P:2", "P:3"])
            for hw in range(8):
                u_proj(hw, hs[hw], "ps01" if hw % 2 == 0 else "ps23",
                       ["P:0", "P:1"] if hw % 2 == 0 else ["P:2", "P:3"])
            for idx, h in enumerate(hs):
                head_o(h)
                if idx >= 1:
                    head_ss(hs[idx - 1])
                    head_ou(hs[idx - 1])
            head_ss(hs[7])
            head_ou(hs[7])

        for d in range(NK):
            sl = d % 2
            so = sl * 32 * 128
            slr = "slab:%d" % sl
            pt_ = "ps01" if d % 2 == 0 else "ps23"
            pres = ["P:0", "P:1"] if d % 2 == 0 else ["P:2", "P:3"]
            tns = PS[pt_]
            em.dma("pool", "m_os%d" % sl,
                   lambda e, d=d, so=so: e.dma_start(
                       out=slab[:, so:so + 16 * 128],
                       in_=wot[d, :, :, :]),
                   writes=[slr], tag=tag + "osl")
            for hf in range(2):
                cs = hf * 512
                for k in range(NK):
                    MM(pt_, 0, 128, cs, cs + 512,
                       slab[:, so + k * 128:so + (k + 1) * 128],
                       oub[:, k * T + cs:k * T + cs + 512],
                       k == 0, k == NK - 1, reads=[slr, "oub:%d" % k],
                       tag=tag + "wo")
            em.op("dve", lambda e, d=d, tns=tns: e.tensor_add(
                  y[:, d * T:(d + 1) * T], y[:, d * T:(d + 1) * T], tns[:, :]),
                  reads=pres + ["y:%d" % d], writes=["y:%d" % d], tag=tag + "ya")

    if "ffn1" in parts:
        emit_ln("ln1")
        emit_ffn(1)
    if "attn" in parts:
        emit_ln("ln2")
        emit_attn()
    if "ffn2" in parts:
        emit_ln("ln3")
        emit_ffn(2)

    em.dma("sp", "m_out", lambda e: e.dma_start(
        out=y_out, in_=y[:, :]),
        reads=YRES, writes=["yout"], tag="store")
    if debug:
        em.dma("sp", "m_out", lambda e: e.dma_start(out=dbg_out, in_=dbg_sb[:, :]),
               reads=["dbg"], writes=["dbgout"], tag="storedbg")

    em.finalize()

    import contextlib
    es = contextlib.ExitStack()
    sems = {}
    for s in Em.STREAMS:
        sems[("s", s)] = es.enter_context(nc.semaphore("sem_" + s))
    for s in em.dma_counts:
        sems[("d", s)] = es.enter_context(nc.semaphore("dma_" + s))
    for s in em.cc_counts:
        sems[("cc", s)] = es.enter_context(nc.semaphore("ccs_" + s))

    eng_of = {"pe": "tensor", "act": "scalar", "dve": "vector", "pool": "gpsimd",
              "sp": "sync"}

    with nc.Block() as block:
        def make_body(sname):
            def body(eng):
                for o in em.ops[sname]:
                    for (k, v) in em.waits[o.gidx]:
                        val = 16 * v if k[0] == "d" else v
                        eng.wait_ge(sems[(k[0], k[1])], val)
                    inst = o.fn(eng)
                    if o.kind == "d":
                        inst.then_inc(sems[("d", o.dma_sem)], 16)
                    elif o.kind == "cc":
                        inst.then_inc(sems[("cc", o.dma_sem)], 1)
                    elif o.marked:
                        inst.then_inc(sems[("s", sname)], 1)
                if sname == "sp":
                    eng.wait_ge(sems[("d", "m_out")], 16 * em.dma_counts["m_out"])
            return body

        for sname in Em.STREAMS:
            getattr(block, eng_of[sname])(make_body(sname))

    es.close()
    return nc, sorted(d_in.keys())


def host_inputs(inp, parts=ALL_PARTS):
    f = np.float32
    w = {}

    def tile4(WT):
        K = WT.shape[0] // 128
        M = WT.shape[1] // 128
        return np.ascontiguousarray(WT.reshape(K, 128, M, 128).transpose(2, 0, 1, 3))

    if "attn" in parts:
        wq = np.asarray(inp["L0_attn_wq"])
        wk = np.asarray(inp["L0_attn_wk"])
        wv = np.asarray(inp["L0_attn_wv"])
        wu = np.asarray(inp["L0_attn_wu"])
        wo = np.asarray(inp["L0_attn_wo"])
        pm = lambda a: np.ascontiguousarray(a.transpose(0, 2, 1, 3))
        w["wqt"] = pm(tile4(np.ascontiguousarray(wq.T)))
        w["wut"] = pm(tile4(np.ascontiguousarray(wu.T)))
        w["wot"] = pm(tile4(np.ascontiguousarray(wo.T)))
        kt = tile4(np.ascontiguousarray(wk.T))
        vt = tile4(np.ascontiguousarray(wv.T))
        w["kvslab"] = pm(np.ascontiguousarray(np.concatenate([kt, vt], axis=1)))
    w["gwt"] = np.ascontiguousarray(
        np.asarray(inp["L0_ffn_gate_w"]).T.reshape(NK, 128, 8).transpose(1, 0, 2))
    w["uat"] = np.ascontiguousarray(
        np.asarray(inp["L0_ffn_up_A"]).transpose(2, 0, 1).reshape(NK, 128, ER)
        .transpose(1, 0, 2))
    w["gat"] = np.ascontiguousarray(
        np.asarray(inp["L0_ffn_gate_A"]).transpose(2, 0, 1).reshape(NK, 128, ER)
        .transpose(1, 0, 2))
    wupT = np.asarray(inp["L0_ffn_w_up"]).T
    wgtT = np.asarray(inp["L0_ffn_w_gate"]).T
    UB = np.asarray(inp["L0_ffn_up_B"]).transpose(0, 2, 1).reshape(ER, H)
    GB = np.asarray(inp["L0_ffn_gate_B"]).transpose(0, 2, 1).reshape(ER, H)
    DA = np.asarray(inp["L0_ffn_down_A"]).transpose(2, 0, 1).reshape(H, ER)
    ffnm = np.empty((NM, 35, 128, 128), f)
    for m in range(NM):
        for k in range(NK):
            ffnm[m, k] = wupT[k * 128:(k + 1) * 128, m * 128:(m + 1) * 128]
            ffnm[m, 16 + k] = wgtT[k * 128:(k + 1) * 128, m * 128:(m + 1) * 128]
        ffnm[m, 32] = UB[:, m * 128:(m + 1) * 128]
        ffnm[m, 33] = GB[:, m * 128:(m + 1) * 128]
        ffnm[m, 34] = DA[m * 128:(m + 1) * 128, :]
    w["ffnb"] = np.ascontiguousarray(ffnm[:, :32].transpose(0, 2, 1, 3))
    w["ffnl"] = np.ascontiguousarray(ffnm[:, 32:].transpose(0, 2, 1, 3))
    wdT = np.asarray(inp["L0_ffn_w_down"]).T
    wdtl = np.empty((NGRP, NK, GM, 128, 128), f)
    for g in range(NGRP):
        for d in range(NK):
            for j in range(GM):
                m = g * GM + j
                wdtl[g, d, j] = wdT[m * 128:(m + 1) * 128, d * 128:(d + 1) * 128]
    w["wdt"] = np.ascontiguousarray(wdtl.transpose(0, 1, 3, 2, 4))
    DB = np.asarray(inp["L0_ffn_down_B"]).transpose(0, 2, 1).reshape(ER, D)
    w["dbt"] = np.ascontiguousarray(DB.reshape(128, NK, 128))
    w["c_ident"] = np.eye(128, dtype=f)
    exp = np.zeros((8, 128), f)
    for e in range(8):
        exp[e, e * 16:(e + 1) * 16] = 2.0
    w["c_expand"] = exp
    return w


def run_parts(inputs, parts, debug=False):
    nc, names = build_nc(parts, debug=debug)
    shared = host_inputs(inputs, parts)
    x = np.asarray(inputs["L0_attn_x"])
    in_maps = []
    for c in range(NCORES):
        b, hl = c // 2, c % 2
        im = dict(shared)
        xt = x[b, hl * T:(hl + 1) * T, :].T.astype(np.float32)
        im["x_fm"] = np.ascontiguousarray(
            xt.reshape(NK, 128, T).transpose(1, 0, 2).reshape(128, NK * T))
        in_maps.append(im)
    res = run_bass_kernel_spmd(nc, in_maps, core_ids=list(range(NCORES)))
    out = np.empty((B, L, D), np.float32)
    for c in range(NCORES):
        b, hl = c // 2, c % 2
        ypm = res.results[c]["y_out"].reshape(128, NK, T)
        out[b, hl * T:(hl + 1) * T, :] = ypm.transpose(2, 1, 0).reshape(T, D)
    if debug:
        global DBG
        DBG = {"c%d" % c: res.results[c]["dbg_out"] for c in range(NCORES)}
    return out


DBG = None


def kernel(**inputs):
    return run_parts(inputs, ALL_PARTS)
